# version 33
# baseline (speedup 1.0000x reference)
import math
import sys
from contextlib import ExitStack

import numpy as np

sys.path.insert(0, "/opt/trn_rl_repo")
import concourse.bass as bass
import concourse.mybir as mybir
from concourse.bass_utils import run_bass_kernel_spmd

F32 = mybir.dt.float32
BF16 = mybir.dt.bfloat16
AF = mybir.ActivationFunctionType
ALU = mybir.AluOpType
AX = mybir.AxisListType

T = 4096
C = 1024
NH = 16
HD = 64
I0 = 2816
H1 = 4096
NCORES = 8
BIGRND = 12582912.0
NEGBIG = -1.0e9

ENGS = ("sync", "vector", "scalar", "tensor", "gpsimd")


class Buf:

    __slots__ = ("t", "wr", "rds", "name")

    def __init__(self, t, name=""):
        self.t = t
        self.name = name
        self.wr = None
        self.rds = []

    def __getitem__(self, idx):
        return self.t[idx]

    def ap(self):
        if isinstance(self.t, bass.AP):
            return self.t
        return self.t.ap()


class P:

    def __init__(self, nc, stack, nch=40, flush_n=4000):
        self.nc = nc
        self.stack = stack
        self.block = stack.enter_context(nc.Block())
        self.sem = {e: stack.enter_context(nc.semaphore(f"s_{e}")) for e in ENGS}
        for i in range(nch):
            self.sem[f"ch{i}"] = stack.enter_context(nc.semaphore(f"ch{i}"))
        self.nch = nch
        self.chmap = {}
        self.val = {e: 0 for e in self.sem}
        self.waited = {e: {} for e in ENGS}
        self.ninstr = 0
        self.pend = []
        self.flush_n = flush_n

    def _emit(self, eng, body):
        self.pend.append((eng, body))
        if len(self.pend) >= self.flush_n:
            self.flush()

    def flush(self):
        by = {e: [] for e in ENGS}
        for eng, body in self.pend:
            by[eng].append(body)
        self.pend = []
        for e in ENGS:
            for b in by[e]:
                getattr(self.block, e)(b)

    def _chan(self, buf):
        key = self.chmap.get(id(buf))
        if key is None:
            i = len(self.chmap)
            assert i < self.nch, "channel semaphore pool exhausted in this phase"
            key = f"ch{i}"
            self.chmap[id(buf)] = key
        return key

    def _need(self, eng, deps):
        need = {}
        waited = self.waited[eng]
        for (e, v, is_dma) in deps:
            if v > waited.get(e, 0) and v > need.get(e, 0):
                need[e] = v
        return need

    def _deps(self, eng, reads, writes, extra, pw=False):
        deps = []
        for b in reads:
            if b.wr is not None:
                deps.append(b.wr)
        for b in writes:
            deps.extend(b.rds)
            if b.wr is not None and not pw:
                deps.append(b.wr)
        deps.extend(extra)
        return self._need(eng, deps)

    def do(self, eng, fn, reads=(), writes=(), sreads=(), deps=(), fence=False, tag=""):
        reads = list(reads) + list(sreads)
        need = self._deps(eng, reads, writes, deps)
        sems = self.sem
        waited = self.waited[eng]
        post = self.val[eng] + 1
        for e, v in need.items():
            waited[e] = max(waited.get(e, 0), v)
        if fence:
            waited[eng] = max(waited.get(eng, 0), post)

        def body(engine):
            for e in sorted(need):
                engine.wait_ge(sems[e], need[e])
            instr = fn(engine)
            instr.then_inc(sems[eng], 1)
            if fence:
                engine.wait_ge(sems[eng], post)

        self._emit(eng, body)
        self.val[eng] = post
        ev = (eng, post, False)
        for b in reads:
            b.rds.append(ev)
        for b in writes:
            b.wr = ev
            b.rds = []
        self.ninstr += 1
        return ev

    def dma(self, eng, out_ap, in_ap, reads=(), writes=(), deps=(), fence=False,
            pw=False, **kw):
        assert not kw.get("transpose") or eng == "sync", "transpose DMA only on sync"
        wl = list(writes)
        reads = list(reads)
        ch = self._chan(wl[0])
        need = self._deps(eng, reads, wl, deps, pw=pw)
        sems = self.sem
        waited = self.waited[eng]
        post = self.val[ch] + 16
        for e, v in need.items():
            waited[e] = max(waited.get(e, 0), v)
        if fence:
            waited[ch] = max(waited.get(ch, 0), post)

        def body(engine):
            for e in sorted(need):
                engine.wait_ge(sems[e], need[e])
            instr = engine.dma_start(out=out_ap, in_=in_ap, **kw)
            instr.then_inc(sems[ch], 16)
            if fence:
                engine.wait_ge(sems[ch], post)

        self._emit(eng, body)
        self.val[ch] = post
        ev = (ch, post, True)
        for b in reads:
            b.rds.append(ev)
        for b in wl:
            b.wr = ev
            if not pw:
                b.rds = []
        self.ninstr += 1
        return ev

    def barrier_ev(self):
        return [(e, v, True) for e, v in self.val.items() if v > 0]

    def phase_barrier(self):
        self.flush()
        vals = {e: v for e, v in self.val.items() if v > 0}
        for eng in ENGS:
            waited = self.waited[eng]
            todo = {e: v for e, v in vals.items() if v > waited.get(e, 0)}
            for e, v in todo.items():
                waited[e] = v

            def body(engine, todo=todo):
                for e, v in sorted(todo.items()):
                    engine.wait_ge(self.sem[e], v)
            getattr(self.block, eng)(body)
        self.chmap.clear()

    def sb(self, name, shape, dtype):
        return Buf(self.stack.enter_context(self.nc.sbuf_tensor(name, shape, dtype)), name)

    def ps(self, name, shape, dtype=F32):
        return Buf(self.stack.enter_context(self.nc.psum_tensor(name, shape, dtype)), name)


def bcast0(ap3, n):
    return bass.AP(ap3.tensor, ap3.offset, list(ap3.ap[:-1]) + [[0, n]])


SCHEMA = {
    "x0": ([T, C], F32, "in"),
    "w_attn": ([C, 3 * C], F32, "in"),
    "b_attn": ([3 * C], F32, "in"),
    "w_proj": ([C, C], F32, "in"),
    "b_proj": ([C], F32, "in"),
    "w_gate": ([I0, C], F32, "in"),
    "w_up": ([I0, C], F32, "in"),
    "w_down": ([C, I0], F32, "in"),
    "qkv_w": ([C, 3 * C], F32, "in"),
    "qkv_b": ([3 * C], F32, "in"),
    "out_w": ([C, C], F32, "in"),
    "out_b": ([C], F32, "in"),
    "w1_w": ([H1, C], F32, "in"),
    "w1_s": ([H1], F32, "in"),
    "w2_w": ([H1, C], F32, "in"),
    "w2_s": ([H1], F32, "in"),
    "h1_w": ([H1, H1], F32, "in"),
    "h1_s": ([H1], F32, "in"),
    "h2_w": ([H1, H1], F32, "in"),
    "h2_s": ([H1], F32, "in"),
    "w3_w": ([C, H1], F32, "in"),
    "w3_s": ([C], F32, "in"),
    "out": ([T, C], F32, "out"),
    "wqg": ([I0, C], BF16, "pw_quant"),
    "wqu": ([I0, C], BF16, "pw_quant"),
    "wqd": ([C, I0], BF16, "pw_quant"),
    "w1D": ([128, 8, 8, 512], BF16, "pw_cast"),
    "w2D": ([128, 8, 8, 512], BF16, "pw_cast"),
    "h1D": ([128, 16, 32, 256], BF16, "pw_cast"),
    "h2D": ([128, 16, 32, 256], BF16, "pw_cast"),
    "w3D": ([128, 32, 1024], BF16, "pw_cast"),
    "muR0": ([128, 64], F32, "p1_l0"),
    "qn0": ([2, 64], F32, "p1_l0"),
    "qn1": ([2, 64], F32, "p1_l1"),
    "qf0": ([C, T], BF16, "p1_l0"),
    "kf0": ([C, T], BF16, "p1_l0"),
    "vt0": ([T, C], BF16, "p1_l0"),
    "xa": ([T, C], F32, "p2_l0"),
    "x1": ([T, C], F32, "p3_l0"),
    "qf1": ([C, T], BF16, "p1_l1"),
    "kf1": ([C, T], BF16, "p1_l1"),
    "vt1": ([T, C], BF16, "p1_l1"),
    "xn1f": ([T // 512, 128, 4096], BF16, "p1_l1"),
    "xb": ([T, C], F32, "p2_l1"),
}

DBG_SCHEMA = {
    "dbg_s": ([128, 4096], F32),
    "dbg_p": ([128, 4096], BF16),
    "dbg_stat": ([128, 64], F32),
    "dbg_at": ([128, 1024], BF16),
    "dbg_atf": ([128, 1024], BF16),
    "dbg_pt": ([128, 4096], BF16),
    "dbg_pv": ([128, 1024], F32),
}

DBG_QT = None
P2_PIPE = True
P6_QROT = ["sync", "scalar", "gpsimd"]
DBG_P6 = False

PHASE_ORDER = ["pw_quant", "pw_cast", "p1_l0", "p2_l0", "p3_l0", "p1_l1", "p2_l1", "p6_l1"]

PHASE_IO = {
    "pw_quant": (["w_gate", "w_up", "w_down"], ["wqg", "wqu", "wqd"]),
    "pw_cast": (["w1_w", "w2_w", "h1_w", "h2_w", "w3_w"], ["w1D", "w2D", "h1D", "h2D", "w3D"]),
    "p1_l0": (["x0", "w_attn", "b_attn"], ["muR0", "qf0", "kf0", "vt0", "qn0"]),
    "p2_l0": (["qf0", "kf0", "vt0", "w_proj", "b_proj", "x0", "qn0"], ["xa"]),
    "p3_l0": (["x0", "muR0", "wqg", "wqu", "wqd", "xa"], ["x1"]),
    "p1_l1": (["x1", "qkv_w", "qkv_b"], ["qf1", "kf1", "vt1", "xn1f", "qn1"]),
    "p2_l1": (["qf1", "kf1", "vt1", "out_w", "out_b", "x1", "qn1"], ["xb"]),
    "p6_l1": (["xn1f", "w1D", "w1_s", "w2D", "w2_s", "h1D", "h1_s", "h2D", "h2_s",
               "w3D", "w3_s", "xb"], ["out"]),
}




def emit_setup(p, st):
    st.ident = p.sb("ident", [128, 128], BF16)
    st.negid = p.sb("negid", [128, 128], BF16)
    st.mask01 = p.sb("mask01", [128, 4 * 512], BF16)
    setup_stack = ExitStack()
    ix = Buf(setup_stack.enter_context(p.nc.sbuf_tensor("ix", [128, 512], F32)), "ix")
    iy = Buf(setup_stack.enter_context(p.nc.sbuf_tensor("iy", [128, 512], F32)), "iy")
    tmp = Buf(setup_stack.enter_context(p.nc.sbuf_tensor("setup_tmp", [128, 512], F32)), "tmp")
    st.setup_stack = setup_stack

    p.do("gpsimd", lambda g: g.iota(ix[:], [[1, 512]], channel_multiplier=0,
                                    allow_small_or_imprecise_dtypes=True), writes=[ix])
    p.do("gpsimd", lambda g: g.iota(iy[:], [[0, 512]], channel_multiplier=1,
                                    allow_small_or_imprecise_dtypes=True), writes=[iy])
    p.do("vector", lambda v: v.tensor_tensor(out=st.ident[:], in0=ix[:, 0:128],
                                             in1=iy[:, 0:128], op=ALU.is_equal),
         reads=[ix, iy], writes=[st.ident])
    p.do("vector", lambda v: v.tensor_scalar(out=st.negid[:], in0=st.ident[:],
                                             scalar1=NEGBIG, scalar2=None, op0=ALU.mult),
         reads=[st.ident], writes=[st.negid])
    for a in range(4):
        p.do("vector", lambda v, a=a: v.tensor_scalar(
            out=tmp[:], in0=iy[:], scalar1=float(a * 128), scalar2=None, op0=ALU.add),
             reads=[iy], writes=[tmp])
        p.do("vector", lambda v, a=a: v.tensor_tensor(
            out=st.mask01[:, a * 512:(a + 1) * 512], in0=ix[:], in1=tmp[:], op=ALU.is_gt),
             reads=[ix, tmp], writes=[st.mask01])




def emit_quant(p, x, out_bf, q, F, eng="vector", sgn_eng="scalar", c0=0):
    B = F // 16
    bm, sch, A, t1, t2 = q["bm"], q["sch"], q["A"], q["t1"], q["t2"]
    x3 = lambda b, o=c0: b.ap()[:, o:o + F].rearrange("p (b k) -> p b k", k=16)
    s3 = lambda b: b.ap()[:, 0:F].rearrange("p (b k) -> p b k", k=16)
    cf = slice(0, F)
    cb = slice(0, B)
    xf = slice(c0, c0 + F)

    p.do("vector", lambda v: v.tensor_reduce(out=bm[:, cb], in_=x3(x), axis=AX.X,
                                             op=ALU.max, apply_absolute_value=True),
         reads=[x], writes=[bm])
    p.do("vector", lambda v: v.tensor_scalar(out=sch[:, cb], in0=bm[:, cb],
                                             scalar1=1.0 / 6.0, scalar2=1.0e-12,
                                             op0=ALU.mult, op1=ALU.max),
         reads=[bm], writes=[sch])
    p.do("vector", lambda v: v.reciprocal(out=bm[:, cb], in_=sch[:, cb]),
         reads=[sch], writes=[bm])
    rb = bcast0(bm.ap().rearrange("p b -> p b ()")[:, cb], 16)
    schb = bcast0(sch.ap().rearrange("p b -> p b ()")[:, cb], 16)

    E = eng
    p.do(E, lambda v: v.tensor_tensor(out=s3(t1), in0=x3(x), in1=rb, op=ALU.mult),
         reads=[x, bm], writes=[t1])
    p.do(sgn_eng, lambda s: s.activation(out=t2[:, cf], in_=t1[:, cf], func=AF.Sign),
         reads=[t1], writes=[t2])
    p.do(E, lambda v: v.tensor_tensor(out=A[:, cf], in0=t1[:, cf], in1=t2[:, cf],
                                      op=ALU.mult),
         reads=[t1, t2], writes=[A])
    p.do(E, lambda v: v.tensor_scalar(out=t1[:, cf], in0=A[:, cf], scalar1=2.0,
                                      scalar2=2.0, op0=ALU.min, op1=ALU.mult),
         reads=[A], writes=[t1])
    p.do(E, lambda v: v.tensor_scalar(out=t1[:, cf], in0=t1[:, cf], scalar1=BIGRND,
                                      scalar2=-BIGRND, op0=ALU.add, op1=ALU.add),
         reads=[t1], writes=[t1])
    p.do(E, lambda v: v.tensor_scalar(out=t1[:, cf], in0=t1[:, cf], scalar1=0.5,
                                      scalar2=None, op0=ALU.mult),
         reads=[t1], writes=[t1])
    for th in (2.5, 3.5, 5.0, 5.0):
        p.do("vector", lambda v, th=th: v.scalar_tensor_tensor(
            out=t1[:, cf], in0=A[:, cf], scalar=th, in1=t1[:, cf],
            op0=ALU.is_gt, op1=ALU.add),
             reads=[A, t1], writes=[t1])
    p.do(E, lambda v: v.tensor_tensor(out=s3(t2), in0=s3(t2), in1=schb, op=ALU.mult),
         reads=[t2, sch], writes=[t2])
    p.do(E, lambda v: v.tensor_tensor(out=out_bf[:, xf], in0=t1[:, cf],
                                      in1=t2[:, cf], op=ALU.mult),
         reads=[t1, t2], writes=[out_bf])


def emit_bcast_row(p, vec_ap_1d, out_buf, psbank, ones1, browf, browbf, n):
    p.dma("gpsimd", browf[0:1, 0:n], vec_ap_1d.rearrange("n -> () n"),
          writes=[browf], fence=True)
    p.do("vector", lambda v: v.tensor_copy(out=browbf[0:1, 0:n], in_=browf[0:1, 0:n]),
         reads=[browf], writes=[browbf])
    for c0 in range(0, n, 512):
        w = min(512, n - c0)
        p.do("tensor", lambda tt, c0=c0, w=w: tt.matmul(
            psbank[:, 0:w], ones1[0:1, :], browbf[0:1, c0:c0 + w],
            start=True, stop=True), reads=[ones1, browbf], writes=[psbank])
        p.do("vector", lambda v, c0=c0, w=w: v.tensor_copy(
            out=out_buf[:, c0:c0 + w], in_=psbank[:, 0:w]),
             reads=[psbank], writes=[out_buf])




def emit_p2(p, t, st, layer):
    nc = p.nc
    if layer == 0:
        qf, kf, vt = t["qf0"], t["kf0"], t["vt0"]
        wname, bname, xin, xout_d = "w_proj", "b_proj", t["x0"], t["xa"]
        scale = 1.0 / 32.0
        heads = 1
        qn_d = t["qn0"]
    else:
        qf, kf, vt = t["qf1"], t["kf1"], t["vt1"]
        wname, bname, xin, xout_d = "out_w", "out_b", t["x1"], t["xb"]
        scale = 1.0 / 8.0
        heads = NH
        qn_d = t["qn1"]

    with ExitStack() as sx:
        E = sx.enter_context
        kf_sb = Buf(E(nc.sbuf_tensor(f"p2{layer}_k", [128, 8 * 4096], BF16)), "p2_k")
        vt_pre = t.get("_p2vt") if layer == 1 else None
        if vt_pre is None:
            vt_sb = Buf(E(nc.sbuf_tensor(f"p2{layer}_v", [128, 32 * 1024], BF16)), "p2_v")
        else:
            vt_sb = vt_pre
        wp_sb = Buf(E(nc.sbuf_tensor(f"p2{layer}_w", [128, 8 * 1024], BF16)), "p2_w")
        bprep = Buf(E(nc.sbuf_tensor(f"p2{layer}_brep", [128, 1024], F32)), "p2_brep")
        q_sb = [Buf(E(nc.sbuf_tensor(f"p2{layer}_q{i}", [128, 8 * 128], BF16)), f"p2_q{i}") for i in range(2)]
        P2b = [Buf(E(nc.sbuf_tensor(f"p2{layer}_P{i}", [128, 4096], BF16)), f"p2_P{i}") for i in range(2)]
        Pt_sb = Buf(E(nc.sbuf_tensor(f"p2{layer}_Pt", [128, 4096], BF16)), "p2_Pt")
        stat = Buf(E(nc.sbuf_tensor(f"p2{layer}_stat", [128, 128], F32)), "p2_stat")
        qn32 = Buf(E(nc.sbuf_tensor(f"p2{layer}_qn32", [128, 32], F32)), "p2_qn32")
        kmrep = Buf(E(nc.sbuf_tensor(f"p2{layer}_kmrep", [128, 16], F32)), "p2_kmrep")
        qnk_sb = Buf(E(nc.sbuf_tensor(f"p2{layer}_qnk", [2, 64], F32)), "p2_qnk")
        km16 = Buf(E(nc.sbuf_tensor(f"p2{layer}_km16", [1, 16], F32)), "p2_km16")
        nbf = Buf(E(nc.sbuf_tensor(f"p2{layer}_nbf", [1, 64], BF16)), "p2_nbf")
        onesr = Buf(E(nc.sbuf_tensor(f"p2{layer}_onesr", [1, 128], BF16)), "p2_onesr")
        at_sb = Buf(E(nc.sbuf_tensor(f"p2{layer}_at", [128, 1024], BF16)), "p2_at")
        atf_sb = Buf(E(nc.sbuf_tensor(f"p2{layer}_atf", [128, 1024], BF16)), "p2_atf")
        xres = [Buf(E(nc.sbuf_tensor(f"p2{layer}_xr{i}", [128, 1024], F32)), f"p2_xr{i}") for i in range(2)]
        xout = [Buf(E(nc.sbuf_tensor(f"p2{layer}_xo{i}", [128, 1024], F32)), f"p2_xo{i}") for i in range(2)]
        ps_s = [Buf(E(nc.psum_tensor(f"p2{layer}_pss{i}", [128, 512], F32)), f"p2_pss{i}") for i in range(4)]
        ps_t = [Buf(E(nc.psum_tensor(f"p2{layer}_pst{i}", [128, 1024], BF16)), f"p2_pst{i}") for i in range(2)]
        ps_o = [Buf(E(nc.psum_tensor(f"p2{layer}_pso{i}", [128, 512], F32)), f"p2_pso{i}") for i in range(2)]

        KVQ = ["sync", "gpsimd", "scalar"]
        for cc in range(8):
            p.dma(KVQ[cc % 3], kf_sb[:, cc * 4096:(cc + 1) * 4096],
                  kf.ap()[cc * 128:(cc + 1) * 128, :], reads=[kf], writes=[kf_sb], pw=True)
        if vt_pre is None:
            for ti in range(32):
                p.dma(KVQ[ti % 3], vt_sb[:, ti * 1024:(ti + 1) * 1024],
                      vt.ap()[ti * 128:(ti + 1) * 128, :], reads=[vt], writes=[vt_sb], pw=True)
        for cc in range(8):
            p.dma("gpsimd", wp_sb[:, cc * 1024:(cc + 1) * 1024],
                  t[wname].ap()[cc * 128:(cc + 1) * 128, :], reads=[t[wname]], writes=[wp_sb], pw=True)
        p.do("gpsimd", lambda g: g.memset(onesr[:], 1.0), writes=[onesr])
        p.dma("gpsimd", P2b[0][0:1, 0:1024], t[bname].ap().rearrange("n -> () n"),
              reads=[t[bname]], writes=[P2b[0]], fence=True)
        for c0 in range(0, C, 512):
            p.do("tensor", lambda tt, c0=c0: tt.matmul(
                ps_o[0][:, 0:512], onesr[0:1, :], P2b[0][0:1, c0:c0 + 512],
                start=True, stop=True), reads=[onesr, P2b[0]], writes=[ps_o[0]])
            p.do("vector", lambda v, c0=c0: v.tensor_copy(
                out=bprep[:, c0:c0 + 512], in_=ps_o[0][:, 0:512]),
                 reads=[ps_o[0]], writes=[bprep])
        p.dma("sync", qnk_sb[:], qn_d.ap(), reads=[qn_d], writes=[qnk_sb])
        if layer == 0:
            p.dma("sync", km16[0:1, 0:1], qn_d.ap()[0:1, 32:33],
                  reads=[qn_d], writes=[km16])
        else:
            km2 = km16.ap()[0:1, 0:16].rearrange("p (c two) -> p c two", two=2)
            for r in range(2):
                p.dma("sync", km2[:, :, r:r + 1],
                      qn_d.ap()[r:r + 1, 32:40].rearrange("p c -> p c ()"),
                      reads=[qn_d], writes=[km16], pw=True,
                      allow_slow_non_contiguous=True)
        p.do("vector", lambda v: v.tensor_copy(out=nbf[0:1, 0:32], in_=qnk_sb[0:1, 0:32]),
             reads=[qnk_sb], writes=[nbf])
        p.do("vector", lambda v: v.tensor_copy(out=nbf[0:1, 32:32 + heads],
                                               in_=km16[0:1, 0:heads]),
             reads=[km16], writes=[nbf])
        p.do("tensor", lambda tt: tt.matmul(ps_o[1][:, 0:32], onesr[0:1, :],
                                            nbf[0:1, 0:32], start=True, stop=True),
             reads=[onesr, nbf], writes=[ps_o[1]])
        p.do("vector", lambda v: v.tensor_copy(out=qn32[:], in_=ps_o[1][:, 0:32]),
             reads=[ps_o[1]], writes=[qn32])
        p.do("tensor", lambda tt: tt.matmul(ps_o[1][:, 0:heads], onesr[0:1, :],
                                            nbf[0:1, 32:32 + heads], start=True, stop=True),
             reads=[onesr, nbf], writes=[ps_o[1]])
        p.do("vector", lambda v: v.tensor_copy(out=kmrep[:, 0:heads], in_=ps_o[1][:, 0:heads]),
             reads=[ps_o[1]], writes=[kmrep])

        qsrc = qf.ap().rearrange("(c p) q -> p c q", p=128)

        units = [(qt, h) for qt in range(32) for h in range(heads)]

        def emit_A(u):
            qt, h = u
            ui = (qt * heads + h) % 8
            par = (qt * heads + h) % 2
            P_sb = P2b[par]
            qb = q_sb[qt % 2]
            nkb = qt // 4 + 1
            maska = qt % 4
            hc = 0 if layer == 0 else h // 2
            pb = 0 if layer == 0 else (h % 2) * 64
            kw = 128 if layer == 0 else 64
            kcol = 0 if layer == 0 else h
            trig = heads // 2 if heads > 1 else 0
            if qt == 0 and h == 0:
                p.dma("sync", qb.ap().rearrange("p (c q) -> p c q", q=128),
                      qsrc[:, :, 0:128], reads=[qf], writes=[qb])
            if h == 0:
                p.dma("sync", xres[qt % 2][:], xin.ap()[qt * 128:(qt + 1) * 128, :],
                      reads=[xin], writes=[xres[qt % 2]])
            if h == trig and qt + 1 < 32:
                nqb = q_sb[(qt + 1) % 2]
                p.dma("sync", nqb.ap().rearrange("p (c q) -> p c q", q=128),
                      qsrc[:, :, (qt + 1) * 128:(qt + 2) * 128], reads=[qf], writes=[nqb])
            if h == 0:
                qn3 = bcast0(qn32.ap().rearrange("p n -> p n ()")[:, qt:qt + 1], heads)
                p.do("vector", lambda v, qn3=qn3: v.tensor_tensor(
                    out=stat.ap()[:, 0:heads].rearrange("p n -> p () n"),
                    in0=qn3,
                    in1=kmrep.ap()[:, 0:heads].rearrange("p n -> p () n"),
                    op=ALU.mult), reads=[qn32, kmrep], writes=[stat])
                p.do("vector", lambda v: v.tensor_scalar(
                    out=stat[:, 0:heads], in0=stat[:, 0:heads],
                    scalar1=-1.02 * scale, scalar2=None, op0=ALU.mult),
                     reads=[stat], writes=[stat])
            for kb in range(nkb):
                bank = ps_s[kb % 4]
                last = (kb == nkb - 1)
                def fsc(tt, kb=kb, bank=bank, last=last, qb=qb, hc=hc,
                        pb=pb, kw=kw, maska=maska):
                    ii = None
                    if layer == 0:
                        for cc in range(8):
                            ii = tt.matmul(
                                bank[:, :],
                                qb[:, cc * 128:(cc + 1) * 128],
                                kf_sb[:, cc * 4096 + kb * 512:cc * 4096 + (kb + 1) * 512],
                                start=(cc == 0), stop=(cc == 7 and not last))
                    else:
                        ii = tt.matmul(
                            bank[:, :],
                            qb[pb:pb + kw, hc * 128:(hc + 1) * 128],
                            kf_sb[pb:pb + kw, hc * 4096 + kb * 512:hc * 4096 + (kb + 1) * 512],
                            start=True, stop=not last)
                    if last:
                        ii = tt.matmul(
                            bank[:, :],
                            st.negid[:],
                            st.mask01[:, maska * 512:(maska + 1) * 512],
                            start=False, stop=True)
                    return ii
                p.do("tensor", fsc, reads=[qb, kf_sb, st.negid, st.mask01], writes=[bank])
                cs = 64 + ui * 8 + kb
                p.do("scalar", lambda s, bank=bank, kb=kb, cs=cs, kcol=kcol, P_sb=P_sb:
                     s.activation(out=P_sb[:, kb * 512:(kb + 1) * 512], in_=bank[:, :],
                                  func=AF.Exp,
                                  scale=scale, bias=stat[:, kcol:kcol + 1],
                                  accum_out=stat[:, cs:cs + 1]),
                     reads=[bank], sreads=[stat], writes=[P_sb, stat])

        def emit_B1(u):
            qt, h = u
            ui = (qt * heads + h) % 8
            par = (qt * heads + h) % 2
            P_sb = P2b[par]
            nkb = qt // 4 + 1
            p.do("scalar", lambda s, ui=ui: s.activation(
                out=stat[:, 40 + ui:41 + ui], in_=stat[:, 64 + ui * 8:65 + ui * 8],
                func=AF.Exp, scale=0.0), reads=[stat], writes=[P_sb, stat])
            p.do("vector", lambda v, ui=ui, nkb=nkb: v.tensor_reduce(
                out=stat[:, 24 + ui:25 + ui], in_=stat[:, 64 + ui * 8:64 + ui * 8 + nkb],
                axis=AX.X, op=ALU.add), reads=[stat], writes=[stat])
            p.do("vector", lambda v, ui=ui: v.reciprocal(
                out=stat[:, 32 + ui:33 + ui], in_=stat[:, 24 + ui:25 + ui]),
                 reads=[stat], writes=[stat])

        def emit_B2(u):
            qt, h = u
            ui = (qt * heads + h) % 8
            par = (qt * heads + h) % 2
            P_sb = P2b[par]
            nkb = qt // 4 + 1
            nkc = nkb * 4
            cl_li = slice(32 + ui, 32 + ui + 1)
            for g0 in range(0, nkc, 8):
                gn = min(8, nkc - g0)
                bank = ps_t[(g0 // 8) % 2]
                def ftp(tt, g0=g0, gn=gn, bank=bank, P_sb=P_sb):
                    ii = None
                    for k in range(gn):
                        ii = tt.transpose(bank[:, k * 128:(k + 1) * 128],
                                          P_sb[:, (g0 + k) * 128:(g0 + k + 1) * 128],
                                          st.ident[:])
                    return ii
                p.do("tensor", ftp, reads=[P_sb, st.ident], writes=[bank])
                p.do("vector", lambda v, g0=g0, gn=gn, bank=bank: v.tensor_copy(
                    out=Pt_sb[:, g0 * 128:(g0 + gn) * 128], in_=bank[:, 0:gn * 128]),
                     reads=[bank], writes=[Pt_sb])
            if layer == 0:
                for half in range(2):
                    bank = ps_o[half]
                    def fpv(tt, half=half, bank=bank, nkc=nkc):
                        ii = None
                        for kc in range(nkc):
                            ii = tt.matmul(
                                bank[:, :], Pt_sb[:, kc * 128:(kc + 1) * 128],
                                vt_sb[:, kc * 1024 + half * 512:kc * 1024 + (half + 1) * 512],
                                start=(kc == 0), stop=(kc == nkc - 1))
                        return ii
                    p.do("tensor", fpv, reads=[Pt_sb, vt_sb], writes=[bank])
                    p.do("vector", lambda v, half=half, bank=bank, cl_li=cl_li: v.tensor_scalar(
                        out=at_sb[:, half * 512:(half + 1) * 512], in0=bank[:],
                        scalar1=stat[:, cl_li], scalar2=None, op0=ALU.mult),
                         reads=[bank], sreads=[stat], writes=[at_sb])
            else:
                bank = ps_o[h % 2]
                def fpv(tt, bank=bank, nkc=nkc, h=h):
                    ii = None
                    for kc in range(nkc):
                        ii = tt.matmul(
                            bank[:, 0:64], Pt_sb[:, kc * 128:(kc + 1) * 128],
                            vt_sb[:, kc * 1024 + h * 64:kc * 1024 + (h + 1) * 64],
                            start=(kc == 0), stop=(kc == nkc - 1))
                    return ii
                p.do("tensor", fpv, reads=[Pt_sb, vt_sb], writes=[bank])
                p.do("vector", lambda v, bank=bank, h=h, cl_li=cl_li: v.tensor_scalar(
                    out=at_sb[:, h * 64:(h + 1) * 64], in0=bank[:, 0:64],
                    scalar1=stat[:, cl_li], scalar2=None, op0=ALU.mult),
                     reads=[bank], sreads=[stat], writes=[at_sb])
            if h == heads - 1:
                bank = ps_t[0]
                def fat(tt, bank=bank):
                    ii = None
                    for cc in range(8):
                        ii = tt.transpose(bank[:, cc * 128:(cc + 1) * 128],
                                          at_sb[:, cc * 128:(cc + 1) * 128], st.ident[:])
                    return ii
                p.do("tensor", fat, reads=[at_sb, st.ident], writes=[bank])
                p.do("vector", lambda v, bank=bank: v.tensor_copy(
                    out=atf_sb[:], in_=bank[:]), reads=[bank], writes=[atf_sb])
                xr = xres[qt % 2]
                xo = xout[qt % 2]
                for half in range(2):
                    bank = ps_o[half]
                    def fpj(tt, half=half, bank=bank):
                        ii = None
                        for cc in range(8):
                            ii = tt.matmul(
                                bank[:, :], atf_sb[:, cc * 128:(cc + 1) * 128],
                                wp_sb[:, cc * 1024 + half * 512:cc * 1024 + (half + 1) * 512],
                                start=(cc == 0), stop=(cc == 7))
                        return ii
                    p.do("tensor", fpj, reads=[atf_sb, wp_sb], writes=[bank])
                    p.do("vector", lambda v, half=half, bank=bank, xo=xo: v.tensor_tensor(
                        out=xo[:, half * 512:(half + 1) * 512], in0=bank[:],
                        in1=bprep[:, half * 512:(half + 1) * 512], op=ALU.add),
                         reads=[bank, bprep], writes=[xo])
                    p.do("vector", lambda v, half=half, xo=xo, xr=xr: v.tensor_tensor(
                        out=xo[:, half * 512:(half + 1) * 512],
                        in0=xo[:, half * 512:(half + 1) * 512],
                        in1=xr[:, half * 512:(half + 1) * 512], op=ALU.add),
                         reads=[xo, xr], writes=[xo])
                p.dma("sync", xout_d.ap()[qt * 128:(qt + 1) * 128, :], xo[:],
                      reads=[xo], writes=[xout_d], pw=True)

        emit_A(units[0])
        for i in range(1, len(units)):
            emit_B1(units[i - 1])
            emit_A(units[i])
            emit_B2(units[i - 1])
        emit_B1(units[-1])
        emit_B2(units[-1])




def build_pw_quant_jobs(p, t, stack):
    nc = p.nc
    xt = [Buf(stack.enter_context(nc.sbuf_tensor(f"pwq_x{i}", [128, I0], F32)), f"pwq_x{i}") for i in range(2)]
    ot = [Buf(stack.enter_context(nc.sbuf_tensor("pwq_o0", [128, I0], BF16)), "pwq_o0")] * 2
    q = {}
    for nm, w in [("bm", I0 // 16), ("sch", I0 // 16), ("A", I0 // 2), ("t1", I0 // 2),
                  ("t2", I0 // 2)]:
        q[nm] = Buf(stack.enter_context(nc.sbuf_tensor(f"pwq_{nm}", [128, w], F32)), nm)

    jobs = []
    joblist = []
    for name, out_name, rows, width in [("w_gate", "wqg", I0, C),
                                        ("w_up", "wqu", I0, C),
                                        ("w_down", "wqd", C, I0)]:
        for r0 in range(0, rows, 128):
            joblist.append((name, out_name, r0, width))
    for i, (name, out_name, r0, width) in enumerate(joblist):
        def job(i=i, name=name, out_name=out_name, r0=r0, width=width):
            xb, ob = xt[i % 2], ot[i % 2]
            src_ap = t[name].ap()[r0:r0 + 128, :]
            p.dma("scalar", xb[:, 0:width], src_ap, reads=[t[name]], writes=[xb])
            for c0 in range(0, width, I0 // 2):
                emit_quant(p, xb, ob, q, min(I0 // 2, width - c0), eng="vector",
                           sgn_eng="scalar", c0=c0)
            p.dma("scalar", t[out_name].ap()[r0:r0 + 128, :], ob[:, 0:width],
                  reads=[ob], writes=[t[out_name]], pw=True)
        jobs.append(job)
    return jobs


def emit_pw_quant(p, t):
    with ExitStack() as stack:
        for job in build_pw_quant_jobs(p, t, stack):
            job()


def emit_pw_cast(p, t, part="all"):
    nc = p.nc
    if "_pwc_tmp" not in t:
        t["_pwc_tmp"] = {
            "w1_w": Buf(nc.dram_tensor("tmp_w1bf", [H1, C], BF16), "tmp_w1bf"),
            "w2_w": Buf(nc.dram_tensor("tmp_w2bf", [H1, C], BF16), "tmp_w2bf"),
            "h1_w": Buf(nc.dram_tensor("tmp_h1bf", [H1, H1], BF16), "tmp_h1bf"),
            "h2_w": Buf(nc.dram_tensor("tmp_h2bf", [H1, H1], BF16), "tmp_h2bf"),
            "w3_w": Buf(nc.dram_tensor("tmp_w3bf", [C, H1], BF16), "tmp_w3bf"),
        }
    tmp = t["_pwc_tmp"]
    if part in ("all", "casts"):
        for name, (rows, cols) in [("w1_w", (H1, C)), ("w2_w", (H1, C)),
                                   ("h1_w", (H1, H1)), ("h2_w", (H1, H1)),
                                   ("w3_w", (C, H1))]:
            nchunk = 4
            step = rows // nchunk
            for k in range(nchunk):
                p.dma("gpsimd", tmp[name].ap()[k * step:(k + 1) * step, :],
                      t[name].ap()[k * step:(k + 1) * step, :],
                      reads=[t[name]], writes=[tmp[name]], pw=True)
    if part in ("all", "transposes"):
        for step in build_pwc_steps(p, t):
            step()


def build_pwc_steps(p, t):
    nc = p.nc
    emit_pw_cast(p, t, part="none")
    tmp = t["_pwc_tmp"]
    if "_pwc_stg" not in t:
        t["_pwc_stack"] = ExitStack()
        t["_pwc_stg"] = [Buf(t["_pwc_stack"].enter_context(
            nc.sbuf_tensor(f"pwc_s{i}", [128, H1], BF16)), f"pwc_s{i}")
            for i in range(3)]
    stg = t["_pwc_stg"]
    steps = []
    si = [0]
    for name, out_name, rows, cols, cw in [("w1_w", "w1D", H1, C, 512),
                                           ("w2_w", "w2D", H1, C, 512),
                                           ("h1_w", "h1D", H1, H1, 256),
                                           ("h2_w", "h2D", H1, H1, 256),
                                           ("w3_w", "w3D", C, H1, 1024)]:
        for ci in range(cols // 128):
            def step(name=name, out_name=out_name, rows=rows, ci=ci, cw=cw):
                sbuf = stg[si[0] % len(stg)]
                si[0] += 1
                p.dma("sync", sbuf[:, 0:rows], tmp[name].ap()[:, ci * 128:(ci + 1) * 128],
                      reads=[tmp[name]], writes=[sbuf], transpose=True)
                if out_name == "w3D":
                    dst = t[out_name].ap()[:, ci, :]
                    src = sbuf[:, 0:rows]
                else:
                    dst = t[out_name].ap()[:, :, ci, :]
                    src = sbuf.ap()[:, 0:rows].rearrange("p (w c) -> p w c", c=cw)
                p.dma("scalar", dst, src, reads=[sbuf], writes=[t[out_name]], pw=True)
            steps.append(step)
    return steps




def emit_p1(p, t, st, layer, feeder=None):
    nc = p.nc
    if layer == 0:
        xin, wname, bname = t["x0"], "w_attn", "b_attn"
        qf, kf, vt = t["qf0"], t["kf0"], t["vt0"]
    else:
        xin, wname, bname = t["x1"], "qkv_w", "qkv_b"
        qf, kf, vt = t["qf1"], t["kf1"], t["vt1"]

    with ExitStack() as sx:
        E = sx.enter_context
        w_sb = Buf(E(nc.sbuf_tensor(f"p1{layer}_w", [128, 8 * 3072], BF16)), "p1_w")
        bias = Buf(E(nc.sbuf_tensor(f"p1{layer}_b", [128, 24], F32)), "p1_b")
        bvrep = Buf(E(nc.sbuf_tensor(f"p1{layer}_bvrep", [128, 1024], F32)), "p1_bvrep")
        ones1 = Buf(E(nc.sbuf_tensor(f"p1{layer}_ones", [1, 128], BF16)), "p1_ones")
        brow = Buf(E(nc.sbuf_tensor(f"p1{layer}_brow", [1, 1024], BF16)), "p1_brow")
        browf = Buf(E(nc.sbuf_tensor(f"p1{layer}_browf", [1, 1024], F32)), "p1_browf")
        xt = [Buf(E(nc.sbuf_tensor(f"p1{layer}_x{i}", [128, 1024], F32)), f"p1_x{i}") for i in range(8)]
        mu = Buf(E(nc.sbuf_tensor(f"p1{layer}_mu", [128, 32], F32)), "p1_mu")
        rs = Buf(E(nc.sbuf_tensor(f"p1{layer}_rs", [128, 32], F32)), "p1_rs")
        sq = Buf(E(nc.sbuf_tensor(f"p1{layer}_sq", [128, 32], F32)), "p1_sq")
        var = Buf(E(nc.sbuf_tensor(f"p1{layer}_var", [128, 32], F32)), "p1_var")
        sqjunk = Buf(E(nc.sbuf_tensor(f"p1{layer}_sqjunk", [128, 1024], F32)), "p1_sqjunk")
        lnjunk = Buf(E(nc.sbuf_tensor(f"p1{layer}_lnjunk", [128, 1], F32)), "p1_lnjunk")
        xjunk = Buf(E(nc.sbuf_tensor(f"p1{layer}_xjunk", [128, 1024], F32)), "p1_xjunk")
        xnb = [Buf(E(nc.sbuf_tensor(f"p1{layer}_xnb{i}", [128, 1024], BF16)), f"p1_xnb{i}") for i in range(2)]
        xnf = [Buf(E(nc.sbuf_tensor(f"p1{layer}_xnf{i}", [128, 4096], BF16)), f"p1_xnf{i}") for i in range(2)]
        ev = [Buf(E(nc.sbuf_tensor(f"p1{layer}_ev{i}", [128, 512], BF16)), f"p1_ev{i}") for i in range(4)]
        vev = [Buf(E(nc.sbuf_tensor(f"p1{layer}_vev{i}", [128, 1024], BF16)), f"p1_vev{i}") for i in range(2)]
        sq2 = Buf(E(nc.sbuf_tensor(f"p1{layer}_sq2", [128, 512], F32)), "p1_sq2")
        hsel = Buf(E(nc.sbuf_tensor(f"p1{layer}_hsel", [128, 3], F32)), "p1_hsel")
        nrmq = Buf(E(nc.sbuf_tensor(f"p1{layer}_nrmq", [1, 32], F32)), "p1_nrmq")
        nrmk = Buf(E(nc.sbuf_tensor(f"p1{layer}_nrmk", [2, 8], F32)), "p1_nrmk")
        kmt = Buf(E(nc.sbuf_tensor(f"p1{layer}_kmt", [2, 2], F32)), "p1_kmt")
        pst = [Buf(E(nc.psum_tensor(f"p1{layer}_pst{i}", [128, 1024], BF16)), f"p1_pst{i}") for i in range(2)]
        psq = [Buf(E(nc.psum_tensor(f"p1{layer}_psq{i}", [128, 512], F32)), f"p1_psq{i}") for i in range(4)]
        psv = [Buf(E(nc.psum_tensor(f"p1{layer}_psv{i}", [128, 512], F32)), f"p1_psv{i}") for i in range(2)]

        for cc in range(8):
            p.dma("gpsimd", w_sb[:, cc * 3072:(cc + 1) * 3072],
                  t[wname].ap()[cc * 128:(cc + 1) * 128, :],
                  reads=[t[wname]], writes=[w_sb], pw=True)
        p.dma("sync", bias[:], t[bname].ap().rearrange("(n p) -> p n", p=128),
              reads=[t[bname]], writes=[bias], allow_slow_non_contiguous=True)
        p.do("gpsimd", lambda g: g.memset(ones1[:], 1.0), writes=[ones1])
        p.do("gpsimd", lambda g: g.memset(hsel[:], 0.0), writes=[hsel])
        p.do("gpsimd", lambda g: g.memset(hsel[:, 0:1], 1.0), writes=[hsel])
        p.do("gpsimd", lambda g: g.memset(hsel.ap()[0:64, 1:2], 1.0), writes=[hsel])
        p.do("gpsimd", lambda g: g.memset(hsel.ap()[64:128, 2:3], 1.0), writes=[hsel])
        p.dma("scalar", browf[0:1, :], t[bname].ap()[2048:3072].rearrange("n -> () n"),
              reads=[t[bname]], writes=[browf], fence=True)
        p.do("vector", lambda v: v.tensor_copy(out=brow[0:1, :], in_=browf[0:1, :]),
             reads=[browf], writes=[brow])
        for half in range(2):
            p.do("tensor", lambda tt, half=half: tt.matmul(
                psq[0][:, :], ones1[0:1, :], brow[0:1, half * 512:(half + 1) * 512],
                start=True, stop=True), reads=[ones1, brow], writes=[psq[0]])
            p.do("vector", lambda v, half=half: v.tensor_copy(
                out=bvrep[:, half * 512:(half + 1) * 512], in_=psq[0][:, :]),
                 reads=[psq[0]], writes=[bvrep])

        nblk = T // 512
        for bi in range(nblk):
            xnf_b = xnf[bi % 2]
            for j in range(4):
                ti = bi * 4 + j
                cl = slice(ti % 32, ti % 32 + 1)
                xtile = xt[ti % 4]
                p.dma("sync", xtile[:], xin.ap()[ti * 128:(ti + 1) * 128, :],
                      reads=[xin], writes=[xtile])
                p.do("vector", lambda v, xtile=xtile: v.tensor_tensor(
                    out=sqjunk[:], in0=xtile[:], in1=xtile[:], op=ALU.mult),
                     reads=[xtile], writes=[sqjunk])
                p.do("vector", lambda v, cl=cl: v.tensor_reduce(
                    out=sq[:, cl], in_=sqjunk[:], axis=AX.X, op=ALU.add),
                     reads=[sqjunk], writes=[sq])
                p.do("vector", lambda v, xtile=xtile, cl=cl: v.tensor_reduce(
                    out=mu[:, cl], in_=xtile[:], axis=AX.X, op=ALU.add),
                     reads=[xtile], writes=[mu])
                p.do("vector", lambda v, cl=cl: v.tensor_scalar(
                    out=mu[:, cl], in0=mu[:, cl], scalar1=1.0 / C, scalar2=None,
                    op0=ALU.mult), reads=[mu], writes=[mu])
                p.do("vector", lambda v, cl=cl: v.scalar_tensor_tensor(
                    out=var[:, cl], in0=mu[:, cl], scalar=1.0, in1=mu[:, cl],
                    op0=ALU.mult, op1=ALU.mult), reads=[mu], writes=[var])
                p.do("vector", lambda v, cl=cl: v.tensor_scalar(
                    out=lnjunk[:], in0=sq[:, cl], scalar1=1.0 / C, scalar2=1.0e-5,
                    op0=ALU.mult, op1=ALU.add), reads=[sq], writes=[lnjunk])
                p.do("vector", lambda v, cl=cl: v.tensor_tensor(
                    out=var[:, cl], in0=lnjunk[:], in1=var[:, cl], op=ALU.subtract),
                     reads=[lnjunk, var], writes=[var])
                p.do("scalar", lambda s, cl=cl: s.activation(
                    out=lnjunk[:], in_=var[:, cl], func=AF.Ln),
                     reads=[var], writes=[lnjunk])
                p.do("scalar", lambda s, cl=cl: s.activation(
                    out=rs[:, cl], in_=lnjunk[:], func=AF.Exp, scale=-0.5),
                     reads=[lnjunk], writes=[rs])
                xnb_t = xnb[ti % 2]
                mb = bcast0(mu.ap().rearrange("p n -> p n ()")[:, cl], 1024)
                rb = bcast0(rs.ap().rearrange("p n -> p n ()")[:, cl], 1024)
                p.do("vector", lambda v, xtile=xtile, rb=rb, xnb_t=xnb_t, cl=cl:
                     v.scalar_tensor_tensor(
                         out=xnb_t.ap().rearrange("p (a f) -> p a f", a=1),
                         in0=xtile.ap().rearrange("p (a f) -> p a f", a=1),
                         scalar=mu[:, cl], in1=rb,
                         op0=ALU.subtract, op1=ALU.mult),
                     reads=[xtile, rs], sreads=[mu], writes=[xnb_t])
                pstb = pst[ti % 2]
                def ftp(tt, xnb_t=xnb_t, pstb=pstb):
                    ii = None
                    for cc in range(8):
                        ii = tt.transpose(pstb[:, cc * 128:(cc + 1) * 128],
                                          xnb_t[:, cc * 128:(cc + 1) * 128], st.ident[:])
                    return ii
                p.do("tensor", ftp, reads=[xnb_t, st.ident], writes=[pstb])
                def fev(v, pstb=pstb, xnf_b=xnf_b, j=j):
                    out = xnf_b.ap().rearrange("p (c t) -> p c t", c=8)[:, :, j * 128:(j + 1) * 128]
                    return v.tensor_copy(out=out,
                                         in_=pstb.ap().rearrange("p (c t) -> p c t", c=8))
                p.do("vector", fev, reads=[pstb], writes=[xnf_b])
                if feeder is not None:
                    feeder(bi * 4 + j)
            if layer == 1:
                p.dma("scalar", t["xn1f"].ap()[bi], xnf_b[:],
                      reads=[xnf_b], writes=[t["xn1f"]], pw=True)
            for nc_i in range(16):
                bank = psq[nc_i % 3]
                def fmm(tt, nc_i=nc_i, bank=bank, xnf_b=xnf_b):
                    ii = None
                    for cc in range(8):
                        ii = tt.matmul(bank[:, :],
                                       w_sb[:, cc * 3072 + nc_i * 128:cc * 3072 + (nc_i + 1) * 128],
                                       xnf_b[:, cc * 512:(cc + 1) * 512],
                                       start=(cc == 0), stop=(cc == 7))
                    return ii
                p.do("tensor", fmm, reads=[w_sb, xnf_b], writes=[bank])
                evt = ev[nc_i % 4]
                p.do("vector", lambda v, bank=bank, evt=evt, nc_i=nc_i: v.tensor_scalar(
                    out=evt[:], in0=bank[:], scalar1=bias[:, nc_i:nc_i + 1],
                    scalar2=None, op0=ALU.add),
                     reads=[bank], sreads=[bias], writes=[evt])
                dst = qf if nc_i < 8 else kf
                row = (nc_i % 8) * 128
                p.dma("sync", dst.ap()[row:row + 128, bi * 512:(bi + 1) * 512], evt[:],
                      reads=[evt], writes=[dst], pw=True)
                ps_n = psq[3]
                p.do("vector", lambda v, evt=evt: v.tensor_tensor(
                    out=sq2[:], in0=evt[:], in1=evt[:], op=ALU.mult),
                     reads=[evt], writes=[sq2])
                if nc_i < 8:
                    p.do("tensor", lambda tt, nc_i=nc_i, ps_n=ps_n: tt.matmul(
                        ps_n.ap()[0:1, :], hsel[:, 0:1], sq2[:],
                        start=(nc_i == 0), stop=(nc_i == 7)),
                         reads=[hsel, sq2], writes=[ps_n])
                    if nc_i == 7:
                        p.do("vector", lambda v, bi=bi, ps_n=ps_n: v.tensor_reduce(
                            out=nrmq.ap()[0:1, bi * 4:(bi + 1) * 4],
                            in_=ps_n.ap()[0:1, :].rearrange("p (g c) -> p g c", g=4),
                            axis=AX.X, op=ALU.max), reads=[ps_n], writes=[nrmq])
                elif layer == 0:
                    p.do("tensor", lambda tt, nc_i=nc_i, ps_n=ps_n: tt.matmul(
                        ps_n.ap()[0:1, :], hsel[:, 0:1], sq2[:],
                        start=(nc_i == 8), stop=(nc_i == 15)),
                         reads=[hsel, sq2], writes=[ps_n])
                    if nc_i == 15:
                        p.do("vector", lambda v, bi=bi, ps_n=ps_n: v.tensor_reduce(
                            out=nrmk.ap()[0:1, bi:bi + 1], in_=ps_n.ap()[0:1, :],
                            axis=AX.X, op=ALU.max), reads=[ps_n], writes=[nrmk])
                else:
                    hc = nc_i - 8
                    p.do("tensor", lambda tt, ps_n=ps_n: tt.matmul(
                        ps_n.ap()[0:2, :], hsel[:, 1:3], sq2[:],
                        start=True, stop=True), reads=[hsel, sq2], writes=[ps_n])
                    if bi == 0:
                        p.do("vector", lambda v, hc=hc, ps_n=ps_n: v.tensor_reduce(
                            out=nrmk.ap()[0:2, hc:hc + 1], in_=ps_n.ap()[0:2, :],
                            axis=AX.X, op=ALU.max), reads=[ps_n], writes=[nrmk])
                    else:
                        p.do("vector", lambda v, ps_n=ps_n: v.tensor_reduce(
                            out=kmt.ap()[0:2, 0:1], in_=ps_n.ap()[0:2, :],
                            axis=AX.X, op=ALU.max), reads=[ps_n], writes=[kmt])
                        p.do("vector", lambda v, hc=hc: v.tensor_tensor(
                            out=nrmk.ap()[0:2, hc:hc + 1], in0=nrmk.ap()[0:2, hc:hc + 1],
                            in1=kmt.ap()[0:2, 0:1], op=ALU.max),
                             reads=[nrmk, kmt], writes=[nrmk])
            for j in range(4):
                for half in range(2):
                    bank = psv[half]
                    def fmmv(tt, j=j, half=half, bank=bank, xnf_b=xnf_b):
                        ii = None
                        for cc in range(8):
                            ii = tt.matmul(bank[:, :],
                                           xnf_b[:, cc * 512 + j * 128:cc * 512 + (j + 1) * 128],
                                           w_sb[:, cc * 3072 + 2048 + half * 512:cc * 3072 + 2048 + (half + 1) * 512],
                                           start=(cc == 0), stop=(cc == 7))
                        return ii
                    p.do("tensor", fmmv, reads=[xnf_b, w_sb], writes=[bank])
                vv = vev[j % 2]
                def fvev(v, vv=vv):
                    v.tensor_tensor(out=vv[:, 0:512], in0=psv[0][:], in1=bvrep[:, 0:512],
                                    op=ALU.add)
                    return v.tensor_tensor(out=vv[:, 512:1024], in0=psv[1][:],
                                           in1=bvrep[:, 512:1024], op=ALU.add)
                p.do("vector", fvev, reads=[psv[0], psv[1], bvrep], writes=[vv])
                row = bi * 512 + j * 128
                p.dma("sync", vt.ap()[row:row + 128, :], vv[:], reads=[vv], writes=[vt], pw=True)
        qn_d = t["qn0"] if layer == 0 else t["qn1"]
        p.do("scalar", lambda s: s.activation(out=nrmq[:], in_=nrmq[:], func=AF.Sqrt),
             reads=[nrmq], writes=[nrmq])
        p.dma("sync", qn_d.ap()[0:1, 0:32], nrmq[:], reads=[nrmq], writes=[qn_d], pw=True)
        if layer == 0:
            p.do("vector", lambda v: v.tensor_reduce(
                out=kmt.ap()[0:1, 0:1], in_=nrmk.ap()[0:1, 0:8], axis=AX.X,
                op=ALU.max), reads=[nrmk], writes=[kmt])
            p.do("scalar", lambda s: s.activation(out=kmt.ap()[0:1, 0:1],
                                                  in_=kmt.ap()[0:1, 0:1], func=AF.Sqrt),
                 reads=[kmt], writes=[kmt])
            p.dma("sync", qn_d.ap()[0:1, 32:33], kmt.ap()[0:1, 0:1],
                  reads=[kmt], writes=[qn_d], pw=True)
        else:
            p.do("scalar", lambda s: s.activation(out=nrmk[:], in_=nrmk[:], func=AF.Sqrt),
                 reads=[nrmk], writes=[nrmk])
            p.dma("sync", qn_d.ap()[0:2, 32:40], nrmk[:], reads=[nrmk], writes=[qn_d], pw=True)
        if layer == 0:
            p.dma("sync", t["muR0"].ap()[:, 0:32], mu[:], reads=[mu], writes=[t["muR0"]], pw=True)
            p.dma("sync", t["muR0"].ap()[:, 32:64], rs[:], reads=[rs], writes=[t["muR0"]], pw=True)




def emit_p3(p, t, st):
    nc = p.nc
    NCH = [(i * 512, min(512, I0 - i * 512)) for i in range((I0 + 511) // 512)]
    with ExitStack() as sx:
        E = sx.enter_context
        wqg = Buf(E(nc.sbuf_tensor("p3_wqg", [128, 8 * I0], BF16)), "p3_wqg")
        wqu = Buf(E(nc.sbuf_tensor("p3_wqu", [128, 8 * I0], BF16)), "p3_wqu")
        wqd = Buf(E(nc.sbuf_tensor("p3_wqd", [128, 22 * 1024], BF16)), "p3_wqd")
        mur = Buf(E(nc.sbuf_tensor("p3_mur", [128, 64], F32)), "p3_mur")
        xt = [Buf(E(nc.sbuf_tensor(f"p3_x0t{i}", [128, 1024], F32)), f"p3_x0t{i}") for i in range(2)]
        xat = [Buf(E(nc.sbuf_tensor(f"p3_xa0{i}", [128, 1024], F32)), f"p3_xa0{i}") for i in range(2)]
        xn32 = Buf(E(nc.sbuf_tensor("p3_xn32", [128, 1024], F32)), "p3_xn32")
        xq = Buf(E(nc.sbuf_tensor("p3_xq", [128, 1024], BF16)), "p3_xq")
        xqf = Buf(E(nc.sbuf_tensor("p3_xqf", [128, 1024], BF16)), "p3_xqf")
        h_sb = Buf(E(nc.sbuf_tensor("p3_h", [128, I0], F32)), "p3_h")
        hq = Buf(E(nc.sbuf_tensor("p3_hq", [128, I0], BF16)), "p3_hq")
        hqf = Buf(E(nc.sbuf_tensor("p3_hqf", [128, I0], BF16)), "p3_hqf")
        gs_sb = Buf(E(nc.sbuf_tensor("p3_gs", [128, 512], F32)), "p3_gs")
        x1t = Buf(E(nc.sbuf_tensor("p3_x1t", [128, 1024], F32)), "p3_x1t")
        q = {}
        for nm, w in [("bm", 88), ("sch", 88), ("A", 1408), ("t1", 1408),
                      ("t2", 1408)]:
            q[nm] = Buf(E(nc.sbuf_tensor(f"p3q_{nm}", [128, w], F32)), nm)
        ps_g = Buf(E(nc.psum_tensor("p3_psg", [128, 512], F32)), "p3_psg")
        ps_u = Buf(E(nc.psum_tensor("p3_psu", [128, 512], F32)), "p3_psu")
        ps_t = [Buf(E(nc.psum_tensor(f"p3_pst{i}", [128, 1024], BF16)), f"p3_pst{i}") for i in range(2)]
        ps_d = [Buf(E(nc.psum_tensor(f"p3_psd{i}", [128, 512], F32)), f"p3_psd{i}") for i in range(2)]

        for cc in range(8):
            p.dma("sync", wqg[:, cc * I0:(cc + 1) * I0],
                  t["wqg"].ap()[:, cc * 128:(cc + 1) * 128],
                  reads=[t["wqg"]], writes=[wqg], transpose=True, pw=True)
            p.dma("sync", wqu[:, cc * I0:(cc + 1) * I0],
                  t["wqu"].ap()[:, cc * 128:(cc + 1) * 128],
                  reads=[t["wqu"]], writes=[wqu], transpose=True, pw=True)
        for ic in range(22):
            p.dma("sync", wqd[:, ic * 1024:(ic + 1) * 1024],
                  t["wqd"].ap()[:, ic * 128:(ic + 1) * 128],
                  reads=[t["wqd"]], writes=[wqd], transpose=True, pw=True)
        p.dma("sync", mur[:], t["muR0"].ap(), reads=[t["muR0"]], writes=[mur])

        for ti in range(32):
            xtile, xat_t = xt[ti % 2], xat[ti % 2]
            p.dma("sync", xtile[:], t["x0"].ap()[ti * 128:(ti + 1) * 128, :],
                  reads=[t["x0"]], writes=[xtile])
            p.dma("scalar", xat_t[:], t["xa"].ap()[ti * 128:(ti + 1) * 128, :],
                  reads=[t["xa"]], writes=[xat_t])
            mb = bcast0(mur.ap().rearrange("p n -> p n ()")[:, ti:ti + 1], 1024)
            rb = bcast0(mur.ap().rearrange("p n -> p n ()")[:, 32 + ti:32 + ti + 1], 1024)
            p.do("vector", lambda v, xtile=xtile, mb=mb: v.tensor_tensor(
                out=xn32.ap().rearrange("p (a f) -> p a f", a=1),
                in0=xtile.ap().rearrange("p (a f) -> p a f", a=1),
                in1=mb, op=ALU.subtract), reads=[xtile, mur], writes=[xn32])
            p.do("vector", lambda v, rb=rb: v.tensor_tensor(
                out=xn32.ap().rearrange("p (a f) -> p a f", a=1),
                in0=xn32.ap().rearrange("p (a f) -> p a f", a=1),
                in1=rb, op=ALU.mult), reads=[xn32, mur], writes=[xn32])
            emit_quant(p, xn32, xq, q, 1024, eng="vector", sgn_eng="scalar")
            if DBG_QT is not None and ti == DBG_QT:
                p.dma("sync", t["dbg_p"].ap()[:, 0:1024], xq[:], reads=[xq], writes=[t["dbg_p"]])
                p.dma("sync", t["dbg_s"].ap()[:, 0:1024], xn32[:], reads=[xn32], writes=[t["dbg_s"]])
            pst = ps_t[0]
            def ftp(tt, pst=pst):
                ii = None
                for cc in range(8):
                    ii = tt.transpose(pst[:, cc * 128:(cc + 1) * 128],
                                      xq[:, cc * 128:(cc + 1) * 128], st.ident[:])
                return ii
            p.do("tensor", ftp, reads=[xq, st.ident], writes=[pst])
            p.do("vector", lambda v, pst=pst: v.tensor_copy(
                out=xqf[:], in_=pst[:]), reads=[pst], writes=[xqf])
            for n0, nw in NCH:
                def fg(tt, n0=n0, nw=nw):
                    ii = None
                    for cc in range(8):
                        ii = tt.matmul(ps_g[:, 0:nw], xqf[:, cc * 128:(cc + 1) * 128],
                                       wqg[:, cc * I0 + n0:cc * I0 + n0 + nw],
                                       start=(cc == 0), stop=(cc == 7))
                    return ii
                p.do("tensor", fg, reads=[xqf, wqg], writes=[ps_g])
                def fu(tt, n0=n0, nw=nw):
                    ii = None
                    for cc in range(8):
                        ii = tt.matmul(ps_u[:, 0:nw], xqf[:, cc * 128:(cc + 1) * 128],
                                       wqu[:, cc * I0 + n0:cc * I0 + n0 + nw],
                                       start=(cc == 0), stop=(cc == 7))
                    return ii
                p.do("tensor", fu, reads=[xqf, wqu], writes=[ps_u])
                p.do("scalar", lambda s, nw=nw: s.activation(
                    out=gs_sb[:, 0:nw], in_=ps_g[:, 0:nw], func=AF.Silu),
                     reads=[ps_g], writes=[gs_sb])
                p.do("vector", lambda v, n0=n0, nw=nw: v.tensor_tensor(
                    out=h_sb[:, n0:n0 + nw], in0=ps_u[:, 0:nw], in1=gs_sb[:, 0:nw],
                    op=ALU.mult), reads=[ps_u, gs_sb], writes=[h_sb])
            if DBG_QT is not None and ti == DBG_QT:
                p.dma("sync", t["dbg_p"].ap()[:, 1024:3840], wqg[:, 0:2816], reads=[wqg], writes=[t["dbg_p"]])
                p.dma("sync", t["dbg_stat"].ap()[:, 0:64], mur[:], reads=[mur], writes=[t["dbg_stat"]])
                p.dma("sync", t["dbg_at"].ap()[:, 0:1024], xqf[:], reads=[xqf], writes=[t["dbg_at"]])
                p.dma("gpsimd", t["dbg_pt"].ap()[:, 0:2816], h_sb[:], reads=[h_sb], writes=[t["dbg_pt"]])
            emit_quant(p, h_sb, hq, q, 1408, eng="vector", sgn_eng="scalar", c0=0)
            emit_quant(p, h_sb, hq, q, 1408, eng="vector", sgn_eng="scalar", c0=1408)
            if DBG_QT is not None and ti == DBG_QT:
                p.dma("sync", t["dbg_atf"].ap()[:, 0:1024], hq[:, 0:1024], reads=[hq], writes=[t["dbg_atf"]])
            for g0 in range(0, 22, 8):
                gn = min(8, 22 - g0)
                pst = ps_t[(g0 // 8) % 2]
                def fth(tt, g0=g0, gn=gn, pst=pst):
                    ii = None
                    for k in range(gn):
                        ii = tt.transpose(pst[:, k * 128:(k + 1) * 128],
                                          hq[:, (g0 + k) * 128:(g0 + k + 1) * 128],
                                          st.ident[:])
                    return ii
                p.do("tensor", fth, reads=[hq, st.ident], writes=[pst])
                p.do("vector", lambda v, g0=g0, gn=gn, pst=pst: v.tensor_copy(
                    out=hqf[:, g0 * 128:(g0 + gn) * 128], in_=pst[:, 0:gn * 128]),
                     reads=[pst], writes=[hqf])
            for half in range(2):
                bank = ps_d[half]
                def fd(tt, half=half, bank=bank):
                    ii = None
                    for ic in range(22):
                        ii = tt.matmul(bank[:, :], hqf[:, ic * 128:(ic + 1) * 128],
                                       wqd[:, ic * 1024 + half * 512:ic * 1024 + (half + 1) * 512],
                                       start=(ic == 0), stop=(ic == 21))
                    return ii
                p.do("tensor", fd, reads=[hqf, wqd], writes=[bank])
                p.do("vector", lambda v, half=half, bank=bank, xat_t=xat_t: v.tensor_tensor(
                    out=x1t[:, half * 512:(half + 1) * 512], in0=bank[:],
                    in1=xat_t[:, half * 512:(half + 1) * 512], op=ALU.add),
                     reads=[bank, xat_t], writes=[x1t])
            p.dma("sync", t["x1"].ap()[ti * 128:(ti + 1) * 128, :], x1t[:],
                  reads=[x1t], writes=[t["x1"]], pw=True)




def emit_p6(p, t, st):
    nc = p.nc
    TB = 1024
    NB = T // TB
    with ExitStack() as sx:
        E = sx.enter_context
        xnf = Buf(E(nc.sbuf_tensor("p6_xnf", [128, 8 * TB], BF16)), "p6_xnf")
        u_sb = Buf(E(nc.sbuf_tensor("p6_u", [128, 32 * TB], BF16)), "p6_u")
        v_sb = Buf(E(nc.sbuf_tensor("p6_v", [128, 32 * TB], BF16)), "p6_v")
        hw_sb = [Buf(E(nc.sbuf_tensor(f"p6_hw{i}", [128, 8192], BF16)), f"p6_hw{i}")
                 for i in range(2)]
        s_sb = Buf(E(nc.sbuf_tensor("p6_s", [128, 4 * 32], F32)), "p6_s")
        s3rep = Buf(E(nc.sbuf_tensor("p6_s3rep", [128, 1024], BF16)), "p6_s3rep")
        gs_sb = Buf(E(nc.sbuf_tensor("p6_gs", [128, 512], F32)), "p6_gs")
        xbt = Buf(E(nc.sbuf_tensor("p6_xbt", [128, 1024], F32)), "p6_xbt")
        outt = Buf(E(nc.sbuf_tensor("p6_outt", [128, 1024], F32)), "p6_outt")
        ps_1 = [Buf(E(nc.psum_tensor(f"p6_ps1{i}", [128, 512], F32)), f"p6_ps1{i}") for i in range(2)]
        ps_2 = [Buf(E(nc.psum_tensor(f"p6_ps2{i}", [128, 512], F32)), f"p6_ps2{i}") for i in range(2)]
        ps_v = [Buf(E(nc.psum_tensor(f"p6_psv{i}", [128, 512], F32)), f"p6_psv{i}") for i in range(2)]
        ps_d = [Buf(E(nc.psum_tensor(f"p6_psd{i}", [128, 512], F32)), f"p6_psd{i}") for i in range(2)]

        for i, nm in enumerate(["w1_s", "w2_s", "h1_s", "h2_s"]):
            p.dma("sync", s_sb[:, i * 32:(i + 1) * 32],
                  t[nm].ap().rearrange("(n p) -> p n", p=128),
                  reads=[t[nm]], writes=[s_sb], allow_slow_non_contiguous=True)
        with ExitStack() as bsx:
            ones1 = Buf(bsx.enter_context(nc.sbuf_tensor("p6_ones", [1, 128], BF16)), "p6_ones")
            browf = Buf(bsx.enter_context(nc.sbuf_tensor("p6_browf", [1, 1024], F32)), "p6_browf")
            browbf = Buf(bsx.enter_context(nc.sbuf_tensor("p6_browb", [1, 1024], BF16)), "p6_browb")
            s3f = Buf(bsx.enter_context(nc.sbuf_tensor("p6_s3f", [128, 1024], F32)), "p6_s3f")
            p.do("gpsimd", lambda g: g.memset(ones1[:], 1.0), writes=[ones1])
            emit_bcast_row(p, t["w3_s"].ap(), s3f, ps_d[0], ones1, browf, browbf, C)
            p.do("vector", lambda v: v.tensor_copy(out=s3rep[:], in_=s3f[:]),
                 reads=[s3f], writes=[s3rep])
            p.phase_barrier()

        QROT = P6_QROT
        qi = [0]

        def qn():
            q = QROT[qi[0] % len(QROT)]
            qi[0] += 1
            return q

        for bi in range(NB):
            for hb in range(2):
                p.dma(qn(),
                      xnf.ap().rearrange("p (cc tb) -> p cc tb", tb=TB)[:, :, hb * 512:(hb + 1) * 512],
                      t["xn1f"].ap()[bi * 2 + hb].rearrange("p (cc tb) -> p cc tb", tb=512),
                      reads=[t["xn1f"]], writes=[xnf], pw=True)
            for hwi in range(8):
                wb = hw_sb[hwi % 2]
                p.dma(qn(), wb[:, 0:4096],
                      t["w1D"].ap()[:, hwi].rearrange("p cc c -> p (cc c)"),
                      reads=[t["w1D"]], writes=[wb], pw=True)
                p.dma(qn(), wb[:, 4096:8192],
                      t["w2D"].ap()[:, hwi].rearrange("p cc c -> p (cc c)"),
                      reads=[t["w2D"]], writes=[wb], pw=True)
                for hch in range(4):
                    h = hwi * 4 + hch
                    for th in range(2):
                        b1, b2 = ps_1[th], ps_2[th]
                        def f1(tt, hch=hch, b1=b1, wb=wb, th=th):
                            ii = None
                            for cc in range(8):
                                ii = tt.matmul(
                                    b1[:, :],
                                    wb[:, cc * 512 + hch * 128:cc * 512 + (hch + 1) * 128],
                                    xnf[:, cc * TB + th * 512:cc * TB + (th + 1) * 512],
                                    start=(cc == 0), stop=(cc == 7))
                            return ii
                        p.do("tensor", f1, reads=[wb, xnf], writes=[b1])
                        def f2(tt, hch=hch, b2=b2, wb=wb, th=th):
                            ii = None
                            for cc in range(8):
                                ii = tt.matmul(
                                    b2[:, :],
                                    wb[:, 4096 + cc * 512 + hch * 128:4096 + cc * 512 + (hch + 1) * 128],
                                    xnf[:, cc * TB + th * 512:cc * TB + (th + 1) * 512],
                                    start=(cc == 0), stop=(cc == 7))
                            return ii
                        p.do("tensor", f2, reads=[wb, xnf], writes=[b2])
                        p.do("scalar", lambda s, h=h, b1=b1: s.activation(
                            out=gs_sb[:, 0:512], in_=b1[:], func=AF.Silu,
                            scale=s_sb[:, h:h + 1]), reads=[b1], sreads=[s_sb],
                             writes=[gs_sb])
                        p.do("vector", lambda v, h=h, th=th, b2=b2: v.scalar_tensor_tensor(
                            out=u_sb[:, h * TB + th * 512:h * TB + (th + 1) * 512], in0=b2[:],
                            scalar=s_sb[:, 32 + h:32 + h + 1], in1=gs_sb[:, 0:512],
                            op0=ALU.mult, op1=ALU.mult), reads=[b2, gs_sb],
                             sreads=[s_sb], writes=[u_sb])
            for stage, (wt_name, src_sb, dst_sb, scol) in enumerate(
                    [("h1D", u_sb, v_sb, 64), ("h2D", v_sb, u_sb, 96)]):
                for how in range(16):
                    hw = hw_sb[how % 2]
                    for g0 in range(0, 32, 16):
                        p.dma(qn(),
                              hw.ap()[:, g0 * 256:(g0 + 16) * 256],
                              t[wt_name].ap()[:, how, g0:g0 + 16, :].rearrange(
                                  "p h c -> p (h c)"),
                              reads=[t[wt_name]], writes=[hw], pw=True)
                    for hoch in range(2):
                        for th in range(2):
                            ho = how * 2 + hoch
                            bank = ps_v[th]
                            def fv(tt, hw=hw, hoch=hoch, th=th, bank=bank, src_sb=src_sb):
                                ii = None
                                for hc in range(32):
                                    ii = tt.matmul(
                                        bank[:, :],
                                        hw[:, hc * 256 + hoch * 128:hc * 256 + (hoch + 1) * 128],
                                        src_sb[:, hc * TB + th * 512:hc * TB + (th + 1) * 512],
                                        start=(hc == 0), stop=(hc == 31))
                                return ii
                            p.do("tensor", fv, reads=[hw, src_sb], writes=[bank])
                            p.do("scalar", lambda s, ho=ho, th=th, bank=bank, dst_sb=dst_sb,
                                 scol=scol: s.activation(
                                     out=dst_sb[:, ho * TB + th * 512:ho * TB + (th + 1) * 512],
                                     in_=bank[:],
                                     func=AF.Gelu, scale=s_sb[:, scol + ho:scol + ho + 1]),
                                 reads=[bank], sreads=[s_sb], writes=[dst_sb])
            for half in range(2):
                for b2 in range(2):
                    p.dma(qn(),
                          hw_sb[b2].ap()[:, 0:8192].rearrange("p (h c) -> p h c", c=512),
                          t["w3D"].ap()[:, b2 * 16:(b2 + 1) * 16,
                                        half * 512:(half + 1) * 512],
                          reads=[t["w3D"]], writes=[hw_sb[b2]], pw=True)
                for tch in range(8):
                    row = bi * TB + tch * 128
                    xb_t = xbt
                    p.dma("scalar", xb_t[:, half * 512:(half + 1) * 512],
                          t["xb"].ap()[row:row + 128, half * 512:(half + 1) * 512],
                          reads=[t["xb"]], writes=[xb_t], pw=True)
                    bank = ps_d[tch % 2]
                    def fd(tt, tch=tch, bank=bank):
                        ii = None
                        for hc in range(32):
                            w3t = hw_sb[hc // 16]
                            ii = tt.matmul(
                                bank[:, :],
                                u_sb[:, hc * TB + tch * 128:hc * TB + (tch + 1) * 128],
                                w3t[:, (hc % 16) * 512:(hc % 16 + 1) * 512],
                                start=(hc == 0), stop=(hc == 31))
                        return ii
                    p.do("tensor", fd, reads=[u_sb, hw_sb[0], hw_sb[1]], writes=[bank])
                    ot = outt
                    p.do("vector", lambda v, half=half, bank=bank, ot=ot: v.tensor_tensor(
                        out=ot[:, half * 512:(half + 1) * 512], in0=bank[:],
                        in1=s3rep[:, half * 512:(half + 1) * 512], op=ALU.mult),
                         reads=[bank, s3rep], writes=[ot])
                    p.do("vector", lambda v, half=half, ot=ot, xb_t=xb_t: v.tensor_tensor(
                        out=ot[:, half * 512:(half + 1) * 512],
                        in0=ot[:, half * 512:(half + 1) * 512],
                        in1=xb_t[:, half * 512:(half + 1) * 512], op=ALU.add),
                         reads=[ot, xb_t], writes=[ot])
                    p.dma("sync", t["out"].ap()[row:row + 128,
                                                half * 512:(half + 1) * 512],
                          ot[:, half * 512:(half + 1) * 512],
                          reads=[ot], writes=[t["out"]], pw=True)




class _St:
    pass


def build_program(phases, dev_extra_outputs=()):
    nc = bass.Bass("TRN2", target_bir_lowering=False)
    produced = set()
    for ph in phases:
        produced.update(PHASE_IO[ph][1])
    needed = set()
    for ph in phases:
        needed.update(PHASE_IO[ph][0])
    t = {}
    for name, (shape, dtype, kind) in SCHEMA.items():
        if kind == "in":
            if name in needed:
                t[name] = Buf(nc.dram_tensor(name, shape, dtype, kind="ExternalInput").ap(), name)
        elif kind == "out":
            if name in produced or name in needed:
                t[name] = Buf(nc.dram_tensor(name, shape, dtype, kind="ExternalOutput").ap(), name)
        else:
            if name in produced and (name in dev_extra_outputs):
                t[name] = Buf(nc.dram_tensor(name, shape, dtype, kind="ExternalOutput").ap(), name)
            elif name in produced:
                t[name] = Buf(nc.dram_tensor(name, shape, dtype).ap(), name)
            elif name in needed:
                t[name] = Buf(nc.dram_tensor(name, shape, dtype, kind="ExternalInput").ap(), name)
    if DBG_QT is not None:
        for name, (shape, dtype) in DBG_SCHEMA.items():
            t[name] = Buf(nc.dram_tensor(name, shape, dtype, kind="ExternalOutput").ap(), name)
    if DBG_P6:
        for name in ("dbg_u", "dbg_v", "dbg_w"):
            t[name] = Buf(nc.dram_tensor(name, [8, 128, 32 * 512], BF16, kind="ExternalOutput").ap(), name)
    with ExitStack() as stack:
        p = P(nc, stack)
        st = _St()
        emit_setup(p, st)
        p.phase_barrier()
        st.setup_stack.close()
        merged_pw = ("pw_quant" in phases and "pw_cast" in phases
                     and "p1_l0" in phases)
        for ph in phases:
            if merged_pw and ph in ("pw_quant", "pw_cast"):
                continue
            if ph == "pw_quant":
                emit_pw_quant(p, t)
            elif ph == "pw_cast":
                emit_pw_cast(p, t, part="all")
            elif ph == "p1_l0":
                if merged_pw:
                    pw_stack = ExitStack()
                    pwq_jobs = build_pw_quant_jobs(p, t, pw_stack)
                    pwc_steps = build_pwc_steps(p, t)

                    def fdr(ti):
                        if ti == 0:
                            emit_pw_cast(p, t, part="casts")
                        for _ in range(2):
                            if pwq_jobs:
                                pwq_jobs.pop(0)()
                        if ti >= 4:
                            for _ in range(4):
                                if pwc_steps:
                                    pwc_steps.pop(0)()
                    emit_p1(p, t, st, 0, feeder=fdr)
                    for job in pwq_jobs:
                        job()
                    for s in pwc_steps:
                        s()
                    p.phase_barrier()
                    t.pop("_pwc_stack").close()
                    t.pop("_pwc_stg")
                    pw_stack.close()
                    continue
                emit_p1(p, t, st, 0)
            elif ph == "p1_l1":
                vt_stack = ExitStack()
                t["_p2vt"] = Buf(vt_stack.enter_context(
                    nc.sbuf_tensor("p2vt_pre", [128, 32 * 1024], BF16)), "p2vt_pre")
                t["_p2vt_stack"] = vt_stack
                VQ = ["sync", "gpsimd", "scalar"]

                def fdr1(ti):
                    if ti % 4 == 0 and ti >= 4:
                        for k2 in range(ti - 4, ti):
                            p.dma(VQ[k2 % 3],
                                  t["_p2vt"][:, k2 * 1024:(k2 + 1) * 1024],
                                  t["vt1"].ap()[k2 * 128:(k2 + 1) * 128, :],
                                  reads=[t["vt1"]], writes=[t["_p2vt"]], pw=True)
                emit_p1(p, t, st, 1, feeder=fdr1)
                for k2 in range(28, 32):
                    p.dma(VQ[k2 % 3],
                          t["_p2vt"][:, k2 * 1024:(k2 + 1) * 1024],
                          t["vt1"].ap()[k2 * 128:(k2 + 1) * 128, :],
                          reads=[t["vt1"]], writes=[t["_p2vt"]], pw=True)
            elif ph == "p2_l0":
                emit_p2(p, t, st, 0)
            elif ph == "p2_l1":
                emit_p2(p, t, st, 1)
            elif ph == "p3_l0":
                emit_p3(p, t, st)
            elif ph == "p6_l1":
                emit_p6(p, t, st)
            else:
                raise NotImplementedError(ph)
            if not (ph == "pw_quant" and "pw_cast" in phases):
                p.phase_barrier()
            if ph == "p2_l1" and "_p2vt_stack" in t:
                t.pop("_p2vt_stack").close()
                t.pop("_p2vt")
            if ph == "pw_cast" and "_pwc_stack" in t:
                t.pop("_pwc_stack").close()
                t.pop("_pwc_stg")
        p.flush()
        final = p.barrier_ev()
        def fin(sy):
            for (e, v, _) in final:
                sy.wait_ge(p.sem[e], v)
            return sy.nop()
        p.block.sync(lambda sy: fin(sy))
    return nc, t


_PROG_CACHE = {}


def _full_program():
    if "full" not in _PROG_CACHE:
        _PROG_CACHE["full"] = build_program(PHASE_ORDER)
    return _PROG_CACHE["full"]


_IN_MAP_NAMES = {
    "w_attn": "L0_attn_w_attn", "b_attn": "L0_attn_b_attn",
    "w_proj": "L0_attn_w_proj", "b_proj": "L0_attn_b_proj",
    "w_gate": "L0_ffn_w_gate", "w_up": "L0_ffn_w_up", "w_down": "L0_ffn_w_down",
    "qkv_w": "L1_attn_qkv_w", "qkv_b": "L1_attn_qkv_b",
    "out_w": "L1_attn_out_w", "out_b": "L1_attn_out_b",
    "w1_w": "L1_ffn_w1_w", "w1_s": "L1_ffn_w1_s",
    "w2_w": "L1_ffn_w2_w", "w2_s": "L1_ffn_w2_s",
    "h1_w": "L1_ffn_h1_w", "h1_s": "L1_ffn_h1_s",
    "h2_w": "L1_ffn_h2_w", "h2_s": "L1_ffn_h2_s",
    "w3_w": "L1_ffn_w3_w", "w3_s": "L1_ffn_w3_s",
}


def kernel(**inputs):
    nc, _ = _full_program()
    x = np.ascontiguousarray(inputs["L0_attn_x"])
    shared = {k: np.ascontiguousarray(inputs[v]) for k, v in _IN_MAP_NAMES.items()}
    in_maps = []
    for core in range(NCORES):
        m = dict(shared)
        m["x0"] = np.ascontiguousarray(x[core])
        in_maps.append(m)
    res = run_bass_kernel_spmd(nc, in_maps, core_ids=list(range(NCORES)))
    return np.stack([r["out"] for r in res.results], axis=0)


# version 34
# speedup vs baseline: 1.0377x; 1.0377x over previous
import math
import sys
from contextlib import ExitStack

import numpy as np

sys.path.insert(0, "/opt/trn_rl_repo")
import concourse.bass as bass
import concourse.mybir as mybir
from concourse.bass_utils import run_bass_kernel_spmd

F32 = mybir.dt.float32
BF16 = mybir.dt.bfloat16
AF = mybir.ActivationFunctionType
ALU = mybir.AluOpType
AX = mybir.AxisListType

T = 4096
C = 1024
NH = 16
HD = 64
I0 = 2816
H1 = 4096
NCORES = 8
BIGRND = 12582912.0
NEGBIG = -1.0e9

ENGS = ("sync", "vector", "scalar", "tensor", "gpsimd")


class Buf:

    __slots__ = ("t", "wr", "rds", "name")

    def __init__(self, t, name=""):
        self.t = t
        self.name = name
        self.wr = None
        self.rds = []

    def __getitem__(self, idx):
        return self.t[idx]

    def ap(self):
        if isinstance(self.t, bass.AP):
            return self.t
        return self.t.ap()


class P:

    def __init__(self, nc, stack, nch=40, flush_n=4000):
        self.nc = nc
        self.stack = stack
        self.block = stack.enter_context(nc.Block())
        self.sem = {e: stack.enter_context(nc.semaphore(f"s_{e}")) for e in ENGS}
        for i in range(nch):
            self.sem[f"ch{i}"] = stack.enter_context(nc.semaphore(f"ch{i}"))
        self.nch = nch
        self.chmap = {}
        self.val = {e: 0 for e in self.sem}
        self.waited = {e: {} for e in ENGS}
        self.ninstr = 0
        self.pend = []
        self.flush_n = flush_n

    def _emit(self, eng, body):
        self.pend.append((eng, body))
        if len(self.pend) >= self.flush_n:
            self.flush()

    def flush(self):
        by = {e: [] for e in ENGS}
        for eng, body in self.pend:
            by[eng].append(body)
        self.pend = []
        for e in ENGS:
            for b in by[e]:
                getattr(self.block, e)(b)

    def _chan(self, buf):
        key = self.chmap.get(id(buf))
        if key is None:
            i = len(self.chmap)
            assert i < self.nch, "channel semaphore pool exhausted in this phase"
            key = f"ch{i}"
            self.chmap[id(buf)] = key
        return key

    def _need(self, eng, deps):
        need = {}
        waited = self.waited[eng]
        for (e, v, is_dma) in deps:
            if v > waited.get(e, 0) and v > need.get(e, 0):
                need[e] = v
        return need

    def _deps(self, eng, reads, writes, extra, pw=False):
        deps = []
        for b in reads:
            if b.wr is not None:
                deps.append(b.wr)
        for b in writes:
            deps.extend(b.rds)
            if b.wr is not None and not pw:
                deps.append(b.wr)
        deps.extend(extra)
        return self._need(eng, deps)

    def do(self, eng, fn, reads=(), writes=(), sreads=(), deps=(), fence=False, tag=""):
        reads = list(reads) + list(sreads)
        need = self._deps(eng, reads, writes, deps)
        sems = self.sem
        waited = self.waited[eng]
        post = self.val[eng] + 1
        for e, v in need.items():
            waited[e] = max(waited.get(e, 0), v)
        if fence:
            waited[eng] = max(waited.get(eng, 0), post)

        def body(engine):
            for e in sorted(need):
                engine.wait_ge(sems[e], need[e])
            instr = fn(engine)
            instr.then_inc(sems[eng], 1)
            if fence:
                engine.wait_ge(sems[eng], post)

        self._emit(eng, body)
        self.val[eng] = post
        ev = (eng, post, False)
        for b in reads:
            b.rds.append(ev)
        for b in writes:
            b.wr = ev
            b.rds = []
        self.ninstr += 1
        return ev

    def dma(self, eng, out_ap, in_ap, reads=(), writes=(), deps=(), fence=False,
            pw=False, **kw):
        assert not kw.get("transpose") or eng == "sync", "transpose DMA only on sync"
        wl = list(writes)
        reads = list(reads)
        ch = self._chan(wl[0])
        need = self._deps(eng, reads, wl, deps, pw=pw)
        sems = self.sem
        waited = self.waited[eng]
        post = self.val[ch] + 16
        for e, v in need.items():
            waited[e] = max(waited.get(e, 0), v)
        if fence:
            waited[ch] = max(waited.get(ch, 0), post)

        def body(engine):
            for e in sorted(need):
                engine.wait_ge(sems[e], need[e])
            instr = engine.dma_start(out=out_ap, in_=in_ap, **kw)
            instr.then_inc(sems[ch], 16)
            if fence:
                engine.wait_ge(sems[ch], post)

        self._emit(eng, body)
        self.val[ch] = post
        ev = (ch, post, True)
        for b in reads:
            b.rds.append(ev)
        for b in wl:
            b.wr = ev
            if not pw:
                b.rds = []
        self.ninstr += 1
        return ev

    def barrier_ev(self):
        return [(e, v, True) for e, v in self.val.items() if v > 0]

    def phase_barrier(self):
        self.flush()
        vals = {e: v for e, v in self.val.items() if v > 0}
        for eng in ENGS:
            waited = self.waited[eng]
            todo = {e: v for e, v in vals.items() if v > waited.get(e, 0)}
            for e, v in todo.items():
                waited[e] = v

            def body(engine, todo=todo):
                for e, v in sorted(todo.items()):
                    engine.wait_ge(self.sem[e], v)
            getattr(self.block, eng)(body)
        self.chmap.clear()

    def sb(self, name, shape, dtype):
        return Buf(self.stack.enter_context(self.nc.sbuf_tensor(name, shape, dtype)), name)

    def ps(self, name, shape, dtype=F32):
        return Buf(self.stack.enter_context(self.nc.psum_tensor(name, shape, dtype)), name)


def bcast0(ap3, n):
    return bass.AP(ap3.tensor, ap3.offset, list(ap3.ap[:-1]) + [[0, n]])


SCHEMA = {
    "x0": ([T, C], F32, "in"),
    "w_attn": ([C, 3 * C], F32, "in"),
    "b_attn": ([3 * C], F32, "in"),
    "w_proj": ([C, C], F32, "in"),
    "b_proj": ([C], F32, "in"),
    "w_gate": ([I0, C], F32, "in"),
    "w_up": ([I0, C], F32, "in"),
    "w_down": ([C, I0], F32, "in"),
    "qkv_w": ([C, 3 * C], F32, "in"),
    "qkv_b": ([3 * C], F32, "in"),
    "out_w": ([C, C], F32, "in"),
    "out_b": ([C], F32, "in"),
    "w1_w": ([H1, C], F32, "in"),
    "w1_s": ([H1], F32, "in"),
    "w2_w": ([H1, C], F32, "in"),
    "w2_s": ([H1], F32, "in"),
    "h1_w": ([H1, H1], F32, "in"),
    "h1_s": ([H1], F32, "in"),
    "h2_w": ([H1, H1], F32, "in"),
    "h2_s": ([H1], F32, "in"),
    "w3_w": ([C, H1], F32, "in"),
    "w3_s": ([C], F32, "in"),
    "out": ([T, C], F32, "out"),
    "wqg": ([I0, C], BF16, "pw_quant"),
    "wqu": ([I0, C], BF16, "pw_quant"),
    "wqd": ([C, I0], BF16, "pw_quant"),
    "w1D": ([128, 8, 8, 512], BF16, "pw_cast"),
    "w2D": ([128, 8, 8, 512], BF16, "pw_cast"),
    "h1D": ([128, 16, 32, 256], BF16, "pw_cast"),
    "h2D": ([128, 16, 32, 256], BF16, "pw_cast"),
    "w3D": ([128, 32, 1024], BF16, "pw_cast"),
    "muR0": ([128, 64], F32, "p1_l0"),
    "qn0": ([2, 64], F32, "p1_l0"),
    "qn1": ([2, 64], F32, "p1_l1"),
    "qf0": ([C, T], BF16, "p1_l0"),
    "kf0": ([C, T], BF16, "p1_l0"),
    "vt0": ([T, C], BF16, "p1_l0"),
    "xa": ([T, C], F32, "p2_l0"),
    "x1": ([T, C], F32, "p3_l0"),
    "qf1": ([C, T], BF16, "p1_l1"),
    "kf1": ([C, T], BF16, "p1_l1"),
    "vt1": ([T, C], BF16, "p1_l1"),
    "xn1f": ([T // 512, 128, 4096], BF16, "p1_l1"),
    "xb": ([T, C], F32, "p2_l1"),
}

DBG_SCHEMA = {
    "dbg_s": ([128, 4096], F32),
    "dbg_p": ([128, 4096], BF16),
    "dbg_stat": ([128, 64], F32),
    "dbg_at": ([128, 1024], BF16),
    "dbg_atf": ([128, 1024], BF16),
    "dbg_pt": ([128, 4096], BF16),
    "dbg_pv": ([128, 1024], F32),
}

DBG_QT = None
P2_PIPE = True
P6_QROT = ["sync", "scalar", "gpsimd"]
DBG_P6 = False

PHASE_ORDER = ["pw_quant", "pw_cast", "p1_l0", "p2_l0", "p3_l0", "p1_l1", "p2_l1", "p6_l1"]

PHASE_IO = {
    "pw_quant": (["w_gate", "w_up", "w_down"], ["wqg", "wqu", "wqd"]),
    "pw_cast": (["w1_w", "w2_w", "h1_w", "h2_w", "w3_w"], ["w1D", "w2D", "h1D", "h2D", "w3D"]),
    "p1_l0": (["x0", "w_attn", "b_attn"], ["muR0", "qf0", "kf0", "vt0", "qn0"]),
    "p2_l0": (["qf0", "kf0", "vt0", "w_proj", "b_proj", "x0", "qn0"], ["xa"]),
    "p3_l0": (["x0", "muR0", "wqg", "wqu", "wqd", "xa"], ["x1"]),
    "p1_l1": (["x1", "qkv_w", "qkv_b"], ["qf1", "kf1", "vt1", "xn1f", "qn1"]),
    "p2_l1": (["qf1", "kf1", "vt1", "out_w", "out_b", "x1", "qn1"], ["xb"]),
    "p6_l1": (["xn1f", "w1D", "w1_s", "w2D", "w2_s", "h1D", "h1_s", "h2D", "h2_s",
               "w3D", "w3_s", "xb"], ["out"]),
}




def emit_setup(p, st):
    st.ident = p.sb("ident", [128, 128], BF16)
    st.negid = p.sb("negid", [128, 128], BF16)
    st.mask01 = p.sb("mask01", [128, 4 * 512], BF16)
    setup_stack = ExitStack()
    ix = Buf(setup_stack.enter_context(p.nc.sbuf_tensor("ix", [128, 512], F32)), "ix")
    iy = Buf(setup_stack.enter_context(p.nc.sbuf_tensor("iy", [128, 512], F32)), "iy")
    tmp = Buf(setup_stack.enter_context(p.nc.sbuf_tensor("setup_tmp", [128, 512], F32)), "tmp")
    st.setup_stack = setup_stack

    p.do("gpsimd", lambda g: g.iota(ix[:], [[1, 512]], channel_multiplier=0,
                                    allow_small_or_imprecise_dtypes=True), writes=[ix])
    p.do("gpsimd", lambda g: g.iota(iy[:], [[0, 512]], channel_multiplier=1,
                                    allow_small_or_imprecise_dtypes=True), writes=[iy])
    p.do("vector", lambda v: v.tensor_tensor(out=st.ident[:], in0=ix[:, 0:128],
                                             in1=iy[:, 0:128], op=ALU.is_equal),
         reads=[ix, iy], writes=[st.ident])
    p.do("vector", lambda v: v.tensor_scalar(out=st.negid[:], in0=st.ident[:],
                                             scalar1=NEGBIG, scalar2=None, op0=ALU.mult),
         reads=[st.ident], writes=[st.negid])
    for a in range(4):
        p.do("vector", lambda v, a=a: v.tensor_scalar(
            out=tmp[:], in0=iy[:], scalar1=float(a * 128), scalar2=None, op0=ALU.add),
             reads=[iy], writes=[tmp])
        p.do("vector", lambda v, a=a: v.tensor_tensor(
            out=st.mask01[:, a * 512:(a + 1) * 512], in0=ix[:], in1=tmp[:], op=ALU.is_gt),
             reads=[ix, tmp], writes=[st.mask01])




def emit_quant(p, x, out_bf, q, F, eng="vector", sgn_eng="scalar", c0=0):
    B = F // 16
    bm, sch, A, t1, t2 = q["bm"], q["sch"], q["A"], q["t1"], q["t2"]
    x3 = lambda b, o=c0: b.ap()[:, o:o + F].rearrange("p (b k) -> p b k", k=16)
    s3 = lambda b: b.ap()[:, 0:F].rearrange("p (b k) -> p b k", k=16)
    cf = slice(0, F)
    cb = slice(0, B)
    xf = slice(c0, c0 + F)

    p.do("vector", lambda v: v.tensor_reduce(out=bm[:, cb], in_=x3(x), axis=AX.X,
                                             op=ALU.max, apply_absolute_value=True),
         reads=[x], writes=[bm])
    p.do("vector", lambda v: v.tensor_scalar(out=sch[:, cb], in0=bm[:, cb],
                                             scalar1=1.0 / 6.0, scalar2=1.0e-12,
                                             op0=ALU.mult, op1=ALU.max),
         reads=[bm], writes=[sch])
    p.do("vector", lambda v: v.reciprocal(out=bm[:, cb], in_=sch[:, cb]),
         reads=[sch], writes=[bm])
    rb = bcast0(bm.ap().rearrange("p b -> p b ()")[:, cb], 16)
    schb = bcast0(sch.ap().rearrange("p b -> p b ()")[:, cb], 16)

    E = eng
    p.do(E, lambda v: v.tensor_tensor(out=s3(t1), in0=x3(x), in1=rb, op=ALU.mult),
         reads=[x, bm], writes=[t1])
    p.do(sgn_eng, lambda s: s.activation(out=t2[:, cf], in_=t1[:, cf], func=AF.Sign),
         reads=[t1], writes=[t2])
    p.do(E, lambda v: v.tensor_tensor(out=A[:, cf], in0=t1[:, cf], in1=t2[:, cf],
                                      op=ALU.mult),
         reads=[t1, t2], writes=[A])
    p.do(E, lambda v: v.tensor_scalar(out=t1[:, cf], in0=A[:, cf], scalar1=2.0,
                                      scalar2=2.0, op0=ALU.min, op1=ALU.mult),
         reads=[A], writes=[t1])
    p.do(E, lambda v: v.tensor_scalar(out=t1[:, cf], in0=t1[:, cf], scalar1=BIGRND,
                                      scalar2=-BIGRND, op0=ALU.add, op1=ALU.add),
         reads=[t1], writes=[t1])
    p.do(E, lambda v: v.tensor_scalar(out=t1[:, cf], in0=t1[:, cf], scalar1=0.5,
                                      scalar2=None, op0=ALU.mult),
         reads=[t1], writes=[t1])
    for th in (2.5, 3.5, 5.0, 5.0):
        p.do("vector", lambda v, th=th: v.scalar_tensor_tensor(
            out=t1[:, cf], in0=A[:, cf], scalar=th, in1=t1[:, cf],
            op0=ALU.is_gt, op1=ALU.add),
             reads=[A, t1], writes=[t1])
    p.do(E, lambda v: v.tensor_tensor(out=s3(t2), in0=s3(t2), in1=schb, op=ALU.mult),
         reads=[t2, sch], writes=[t2])
    p.do(E, lambda v: v.tensor_tensor(out=out_bf[:, xf], in0=t1[:, cf],
                                      in1=t2[:, cf], op=ALU.mult),
         reads=[t1, t2], writes=[out_bf])


def emit_bcast_row(p, vec_ap_1d, out_buf, psbank, ones1, browf, browbf, n):
    p.dma("gpsimd", browf[0:1, 0:n], vec_ap_1d.rearrange("n -> () n"),
          writes=[browf], fence=True)
    p.do("vector", lambda v: v.tensor_copy(out=browbf[0:1, 0:n], in_=browf[0:1, 0:n]),
         reads=[browf], writes=[browbf])
    for c0 in range(0, n, 512):
        w = min(512, n - c0)
        p.do("tensor", lambda tt, c0=c0, w=w: tt.matmul(
            psbank[:, 0:w], ones1[0:1, :], browbf[0:1, c0:c0 + w],
            start=True, stop=True), reads=[ones1, browbf], writes=[psbank])
        p.do("vector", lambda v, c0=c0, w=w: v.tensor_copy(
            out=out_buf[:, c0:c0 + w], in_=psbank[:, 0:w]),
             reads=[psbank], writes=[out_buf])




def emit_p2(p, t, st, layer):
    nc = p.nc
    if layer == 0:
        qf, kf, vt = t["qf0"], t["kf0"], t["vt0"]
        wname, bname, xin, xout_d = "w_proj", "b_proj", t["x0"], t["xa"]
        scale = 1.0 / 32.0
        heads = 1
        qn_d = t["qn0"]
    else:
        qf, kf, vt = t["qf1"], t["kf1"], t["vt1"]
        wname, bname, xin, xout_d = "out_w", "out_b", t["x1"], t["xb"]
        scale = 1.0 / 8.0
        heads = NH
        qn_d = t["qn1"]

    with ExitStack() as sx:
        E = sx.enter_context
        kf_sb = Buf(E(nc.sbuf_tensor(f"p2{layer}_k", [128, 8 * 4096], BF16)), "p2_k")
        vt_pre = t.get("_p2vt") if layer == 1 else None
        if vt_pre is None:
            vt_sb = Buf(E(nc.sbuf_tensor(f"p2{layer}_v", [128, 32 * 1024], BF16)), "p2_v")
        else:
            vt_sb = vt_pre
        wp_sb = Buf(E(nc.sbuf_tensor(f"p2{layer}_w", [128, 8 * 1024], BF16)), "p2_w")
        bprep = Buf(E(nc.sbuf_tensor(f"p2{layer}_brep", [128, 1024], F32)), "p2_brep")
        q_sb = [Buf(E(nc.sbuf_tensor(f"p2{layer}_q{i}", [128, 8 * 128], BF16)), f"p2_q{i}") for i in range(2)]
        P2b = [Buf(E(nc.sbuf_tensor(f"p2{layer}_P{i}", [128, 4096], BF16)), f"p2_P{i}") for i in range(2)]
        Pt_sb = Buf(E(nc.sbuf_tensor(f"p2{layer}_Pt", [128, 4096], BF16)), "p2_Pt")
        stat = Buf(E(nc.sbuf_tensor(f"p2{layer}_stat", [128, 128], F32)), "p2_stat")
        qn32 = Buf(E(nc.sbuf_tensor(f"p2{layer}_qn32", [128, 32], F32)), "p2_qn32")
        kmrep = Buf(E(nc.sbuf_tensor(f"p2{layer}_kmrep", [128, 16], F32)), "p2_kmrep")
        qnk_sb = Buf(E(nc.sbuf_tensor(f"p2{layer}_qnk", [2, 64], F32)), "p2_qnk")
        km16 = Buf(E(nc.sbuf_tensor(f"p2{layer}_km16", [1, 16], F32)), "p2_km16")
        nbf = Buf(E(nc.sbuf_tensor(f"p2{layer}_nbf", [1, 64], BF16)), "p2_nbf")
        onesr = Buf(E(nc.sbuf_tensor(f"p2{layer}_onesr", [1, 128], BF16)), "p2_onesr")
        at_sb = Buf(E(nc.sbuf_tensor(f"p2{layer}_at", [128, 1024], BF16)), "p2_at")
        atf_sb = Buf(E(nc.sbuf_tensor(f"p2{layer}_atf", [128, 1024], BF16)), "p2_atf")
        xres = [Buf(E(nc.sbuf_tensor(f"p2{layer}_xr{i}", [128, 1024], F32)), f"p2_xr{i}") for i in range(2)]
        xout = [Buf(E(nc.sbuf_tensor(f"p2{layer}_xo{i}", [128, 1024], F32)), f"p2_xo{i}") for i in range(2)]
        ps_s = [Buf(E(nc.psum_tensor(f"p2{layer}_pss{i}", [128, 512], F32)), f"p2_pss{i}") for i in range(4)]
        ps_t = [Buf(E(nc.psum_tensor(f"p2{layer}_pst{i}", [128, 1024], BF16)), f"p2_pst{i}") for i in range(2)]
        ps_o = [Buf(E(nc.psum_tensor(f"p2{layer}_pso{i}", [128, 512], F32)), f"p2_pso{i}") for i in range(2)]

        KVQ = ["sync", "gpsimd", "scalar"]
        for cc in range(8):
            p.dma(KVQ[cc % 3], kf_sb[:, cc * 4096:(cc + 1) * 4096],
                  kf.ap()[cc * 128:(cc + 1) * 128, :], reads=[kf], writes=[kf_sb], pw=True)
        if vt_pre is None:
            for ti in range(32):
                p.dma(KVQ[ti % 3], vt_sb[:, ti * 1024:(ti + 1) * 1024],
                      vt.ap()[ti * 128:(ti + 1) * 128, :], reads=[vt], writes=[vt_sb], pw=True)
        for cc in range(8):
            p.dma("gpsimd", wp_sb[:, cc * 1024:(cc + 1) * 1024],
                  t[wname].ap()[cc * 128:(cc + 1) * 128, :], reads=[t[wname]], writes=[wp_sb], pw=True)
        p.do("gpsimd", lambda g: g.memset(onesr[:], 1.0), writes=[onesr])
        p.dma("gpsimd", P2b[0][0:1, 0:1024], t[bname].ap().rearrange("n -> () n"),
              reads=[t[bname]], writes=[P2b[0]], fence=True)
        for c0 in range(0, C, 512):
            p.do("tensor", lambda tt, c0=c0: tt.matmul(
                ps_o[0][:, 0:512], onesr[0:1, :], P2b[0][0:1, c0:c0 + 512],
                start=True, stop=True), reads=[onesr, P2b[0]], writes=[ps_o[0]])
            p.do("vector", lambda v, c0=c0: v.tensor_copy(
                out=bprep[:, c0:c0 + 512], in_=ps_o[0][:, 0:512]),
                 reads=[ps_o[0]], writes=[bprep])
        p.dma("sync", qnk_sb[:], qn_d.ap(), reads=[qn_d], writes=[qnk_sb])
        if layer == 0:
            p.dma("sync", km16[0:1, 0:1], qn_d.ap()[0:1, 32:33],
                  reads=[qn_d], writes=[km16])
        else:
            km2 = km16.ap()[0:1, 0:16].rearrange("p (c two) -> p c two", two=2)
            for r in range(2):
                p.dma("sync", km2[:, :, r:r + 1],
                      qn_d.ap()[r:r + 1, 32:40].rearrange("p c -> p c ()"),
                      reads=[qn_d], writes=[km16], pw=True,
                      allow_slow_non_contiguous=True)
        p.do("vector", lambda v: v.tensor_copy(out=nbf[0:1, 0:32], in_=qnk_sb[0:1, 0:32]),
             reads=[qnk_sb], writes=[nbf])
        p.do("vector", lambda v: v.tensor_copy(out=nbf[0:1, 32:32 + heads],
                                               in_=km16[0:1, 0:heads]),
             reads=[km16], writes=[nbf])
        p.do("tensor", lambda tt: tt.matmul(ps_o[1][:, 0:32], onesr[0:1, :],
                                            nbf[0:1, 0:32], start=True, stop=True),
             reads=[onesr, nbf], writes=[ps_o[1]])
        p.do("vector", lambda v: v.tensor_copy(out=qn32[:], in_=ps_o[1][:, 0:32]),
             reads=[ps_o[1]], writes=[qn32])
        p.do("tensor", lambda tt: tt.matmul(ps_o[1][:, 0:heads], onesr[0:1, :],
                                            nbf[0:1, 32:32 + heads], start=True, stop=True),
             reads=[onesr, nbf], writes=[ps_o[1]])
        p.do("vector", lambda v: v.tensor_copy(out=kmrep[:, 0:heads], in_=ps_o[1][:, 0:heads]),
             reads=[ps_o[1]], writes=[kmrep])

        qsrc = qf.ap().rearrange("(c p) q -> p c q", p=128)

        units = [(qt, h) for qt in range(32) for h in range(heads)]

        def emit_A(u):
            qt, h = u
            ui = (qt * heads + h) % 8
            par = (qt * heads + h) % 2
            P_sb = P2b[par]
            qb = q_sb[qt % 2]
            nkb = qt // 4 + 1
            maska = qt % 4
            hc = 0 if layer == 0 else h // 2
            pb = 0 if layer == 0 else (h % 2) * 64
            kw = 128 if layer == 0 else 64
            kcol = 0 if layer == 0 else h
            trig = heads // 2 if heads > 1 else 0
            if qt == 0 and h == 0:
                p.dma("sync", qb.ap().rearrange("p (c q) -> p c q", q=128),
                      qsrc[:, :, 0:128], reads=[qf], writes=[qb])
            if h == 0:
                p.dma("sync", xres[qt % 2][:], xin.ap()[qt * 128:(qt + 1) * 128, :],
                      reads=[xin], writes=[xres[qt % 2]])
            if h == trig and qt + 1 < 32:
                nqb = q_sb[(qt + 1) % 2]
                p.dma("sync", nqb.ap().rearrange("p (c q) -> p c q", q=128),
                      qsrc[:, :, (qt + 1) * 128:(qt + 2) * 128], reads=[qf], writes=[nqb])
            if h == 0:
                qn3 = bcast0(qn32.ap().rearrange("p n -> p n ()")[:, qt:qt + 1], heads)
                p.do("vector", lambda v, qn3=qn3: v.tensor_tensor(
                    out=stat.ap()[:, 0:heads].rearrange("p n -> p () n"),
                    in0=qn3,
                    in1=kmrep.ap()[:, 0:heads].rearrange("p n -> p () n"),
                    op=ALU.mult), reads=[qn32, kmrep], writes=[stat])
                p.do("vector", lambda v: v.tensor_scalar(
                    out=stat[:, 0:heads], in0=stat[:, 0:heads],
                    scalar1=-1.02 * scale, scalar2=None, op0=ALU.mult),
                     reads=[stat], writes=[stat])
            for kb in range(nkb):
                bank = ps_s[kb % 4]
                last = (kb == nkb - 1)
                def fsc(tt, kb=kb, bank=bank, last=last, qb=qb, hc=hc,
                        pb=pb, kw=kw, maska=maska):
                    ii = None
                    if layer == 0:
                        for cc in range(8):
                            ii = tt.matmul(
                                bank[:, :],
                                qb[:, cc * 128:(cc + 1) * 128],
                                kf_sb[:, cc * 4096 + kb * 512:cc * 4096 + (kb + 1) * 512],
                                start=(cc == 0), stop=(cc == 7 and not last))
                    else:
                        ii = tt.matmul(
                            bank[:, :],
                            qb[pb:pb + kw, hc * 128:(hc + 1) * 128],
                            kf_sb[pb:pb + kw, hc * 4096 + kb * 512:hc * 4096 + (kb + 1) * 512],
                            start=True, stop=not last)
                    if last:
                        ii = tt.matmul(
                            bank[:, :],
                            st.negid[:],
                            st.mask01[:, maska * 512:(maska + 1) * 512],
                            start=False, stop=True)
                    return ii
                p.do("tensor", fsc, reads=[qb, kf_sb, st.negid, st.mask01], writes=[bank])
                cs = 64 + ui * 8 + kb
                p.do("scalar", lambda s, bank=bank, kb=kb, cs=cs, kcol=kcol, P_sb=P_sb:
                     s.activation(out=P_sb[:, kb * 512:(kb + 1) * 512], in_=bank[:, :],
                                  func=AF.Exp,
                                  scale=scale, bias=stat[:, kcol:kcol + 1],
                                  accum_out=stat[:, cs:cs + 1]),
                     reads=[bank], sreads=[stat], writes=[P_sb, stat])

        def emit_B1(u):
            qt, h = u
            ui = (qt * heads + h) % 8
            par = (qt * heads + h) % 2
            P_sb = P2b[par]
            nkb = qt // 4 + 1
            p.do("vector", lambda v, ui=ui, nkb=nkb: v.tensor_reduce(
                out=stat[:, 24 + ui:25 + ui], in_=stat[:, 64 + ui * 8:64 + ui * 8 + nkb],
                axis=AX.X, op=ALU.add), reads=[stat], writes=[stat])
            p.do("vector", lambda v, ui=ui: v.reciprocal(
                out=stat[:, 32 + ui:33 + ui], in_=stat[:, 24 + ui:25 + ui]),
                 reads=[stat], writes=[stat])

        def emit_B2(u):
            qt, h = u
            ui = (qt * heads + h) % 8
            par = (qt * heads + h) % 2
            P_sb = P2b[par]
            nkb = qt // 4 + 1
            nkc = nkb * 4
            cl_li = slice(32 + ui, 32 + ui + 1)
            for g0 in range(0, nkc, 8):
                gn = min(8, nkc - g0)
                bank = ps_t[(g0 // 8) % 2]
                def ftp(tt, g0=g0, gn=gn, bank=bank, P_sb=P_sb):
                    ii = None
                    for k in range(gn):
                        ii = tt.transpose(bank[:, k * 128:(k + 1) * 128],
                                          P_sb[:, (g0 + k) * 128:(g0 + k + 1) * 128],
                                          st.ident[:])
                    return ii
                p.do("tensor", ftp, reads=[P_sb, st.ident], writes=[bank])
                p.do("vector", lambda v, g0=g0, gn=gn, bank=bank: v.tensor_copy(
                    out=Pt_sb[:, g0 * 128:(g0 + gn) * 128], in_=bank[:, 0:gn * 128]),
                     reads=[bank], writes=[Pt_sb])
            if layer == 0:
                for half in range(2):
                    bank = ps_o[half]
                    def fpv(tt, half=half, bank=bank, nkc=nkc):
                        ii = None
                        for kc in range(nkc):
                            ii = tt.matmul(
                                bank[:, :], Pt_sb[:, kc * 128:(kc + 1) * 128],
                                vt_sb[:, kc * 1024 + half * 512:kc * 1024 + (half + 1) * 512],
                                start=(kc == 0), stop=(kc == nkc - 1))
                        return ii
                    p.do("tensor", fpv, reads=[Pt_sb, vt_sb], writes=[bank])
                    p.do("vector", lambda v, half=half, bank=bank, cl_li=cl_li: v.tensor_scalar(
                        out=at_sb[:, half * 512:(half + 1) * 512], in0=bank[:],
                        scalar1=stat[:, cl_li], scalar2=None, op0=ALU.mult),
                         reads=[bank], sreads=[stat], writes=[at_sb])
            else:
                bank = ps_o[h % 2]
                def fpv(tt, bank=bank, nkc=nkc, h=h):
                    ii = None
                    for kc in range(nkc):
                        ii = tt.matmul(
                            bank[:, 0:64], Pt_sb[:, kc * 128:(kc + 1) * 128],
                            vt_sb[:, kc * 1024 + h * 64:kc * 1024 + (h + 1) * 64],
                            start=(kc == 0), stop=(kc == nkc - 1))
                    return ii
                p.do("tensor", fpv, reads=[Pt_sb, vt_sb], writes=[bank])
                p.do("vector", lambda v, bank=bank, h=h, cl_li=cl_li: v.tensor_scalar(
                    out=at_sb[:, h * 64:(h + 1) * 64], in0=bank[:, 0:64],
                    scalar1=stat[:, cl_li], scalar2=None, op0=ALU.mult),
                     reads=[bank], sreads=[stat], writes=[at_sb])
            if h == heads - 1:
                bank = ps_t[0]
                def fat(tt, bank=bank):
                    ii = None
                    for cc in range(8):
                        ii = tt.transpose(bank[:, cc * 128:(cc + 1) * 128],
                                          at_sb[:, cc * 128:(cc + 1) * 128], st.ident[:])
                    return ii
                p.do("tensor", fat, reads=[at_sb, st.ident], writes=[bank])
                p.do("vector", lambda v, bank=bank: v.tensor_copy(
                    out=atf_sb[:], in_=bank[:]), reads=[bank], writes=[atf_sb])
                xr = xres[qt % 2]
                xo = xout[qt % 2]
                for half in range(2):
                    bank = ps_o[half]
                    def fpj(tt, half=half, bank=bank):
                        ii = None
                        for cc in range(8):
                            ii = tt.matmul(
                                bank[:, :], atf_sb[:, cc * 128:(cc + 1) * 128],
                                wp_sb[:, cc * 1024 + half * 512:cc * 1024 + (half + 1) * 512],
                                start=(cc == 0), stop=(cc == 7))
                        return ii
                    p.do("tensor", fpj, reads=[atf_sb, wp_sb], writes=[bank])
                    p.do("vector", lambda v, half=half, bank=bank, xo=xo: v.tensor_tensor(
                        out=xo[:, half * 512:(half + 1) * 512], in0=bank[:],
                        in1=bprep[:, half * 512:(half + 1) * 512], op=ALU.add),
                         reads=[bank, bprep], writes=[xo])
                    p.do("vector", lambda v, half=half, xo=xo, xr=xr: v.tensor_tensor(
                        out=xo[:, half * 512:(half + 1) * 512],
                        in0=xo[:, half * 512:(half + 1) * 512],
                        in1=xr[:, half * 512:(half + 1) * 512], op=ALU.add),
                         reads=[xo, xr], writes=[xo])
                p.dma("sync", xout_d.ap()[qt * 128:(qt + 1) * 128, :], xo[:],
                      reads=[xo], writes=[xout_d], pw=True)

        emit_A(units[0])
        for i in range(1, len(units)):
            emit_B1(units[i - 1])
            emit_A(units[i])
            emit_B2(units[i - 1])
        emit_B1(units[-1])
        emit_B2(units[-1])




def build_pw_quant_jobs(p, t, stack):
    nc = p.nc
    xt = [Buf(stack.enter_context(nc.sbuf_tensor(f"pwq_x{i}", [128, I0], F32)), f"pwq_x{i}") for i in range(2)]
    ot = [Buf(stack.enter_context(nc.sbuf_tensor("pwq_o0", [128, I0], BF16)), "pwq_o0")] * 2
    q = {}
    for nm, w in [("bm", I0 // 16), ("sch", I0 // 16), ("A", I0 // 2), ("t1", I0 // 2),
                  ("t2", I0 // 2)]:
        q[nm] = Buf(stack.enter_context(nc.sbuf_tensor(f"pwq_{nm}", [128, w], F32)), nm)

    jobs = []
    joblist = []
    for name, out_name, rows, width in [("w_gate", "wqg", I0, C),
                                        ("w_up", "wqu", I0, C),
                                        ("w_down", "wqd", C, I0)]:
        for r0 in range(0, rows, 128):
            joblist.append((name, out_name, r0, width))
    for i, (name, out_name, r0, width) in enumerate(joblist):
        def job(i=i, name=name, out_name=out_name, r0=r0, width=width):
            xb, ob = xt[i % 2], ot[i % 2]
            src_ap = t[name].ap()[r0:r0 + 128, :]
            p.dma("scalar", xb[:, 0:width], src_ap, reads=[t[name]], writes=[xb])
            for c0 in range(0, width, I0 // 2):
                emit_quant(p, xb, ob, q, min(I0 // 2, width - c0), eng="vector",
                           sgn_eng="scalar", c0=c0)
            p.dma("scalar", t[out_name].ap()[r0:r0 + 128, :], ob[:, 0:width],
                  reads=[ob], writes=[t[out_name]], pw=True)
        jobs.append(job)
    return jobs


def emit_pw_quant(p, t):
    with ExitStack() as stack:
        for job in build_pw_quant_jobs(p, t, stack):
            job()


def emit_pw_cast(p, t, part="all"):
    nc = p.nc
    if "_pwc_tmp" not in t:
        t["_pwc_tmp"] = {
            "w1_w": Buf(nc.dram_tensor("tmp_w1bf", [H1, C], BF16), "tmp_w1bf"),
            "w2_w": Buf(nc.dram_tensor("tmp_w2bf", [H1, C], BF16), "tmp_w2bf"),
            "h1_w": Buf(nc.dram_tensor("tmp_h1bf", [H1, H1], BF16), "tmp_h1bf"),
            "h2_w": Buf(nc.dram_tensor("tmp_h2bf", [H1, H1], BF16), "tmp_h2bf"),
            "w3_w": Buf(nc.dram_tensor("tmp_w3bf", [C, H1], BF16), "tmp_w3bf"),
        }
    tmp = t["_pwc_tmp"]
    if part in ("all", "casts"):
        for name, (rows, cols) in [("w1_w", (H1, C)), ("w2_w", (H1, C)),
                                   ("h1_w", (H1, H1)), ("h2_w", (H1, H1)),
                                   ("w3_w", (C, H1))]:
            nchunk = 4
            step = rows // nchunk
            for k in range(nchunk):
                p.dma("gpsimd", tmp[name].ap()[k * step:(k + 1) * step, :],
                      t[name].ap()[k * step:(k + 1) * step, :],
                      reads=[t[name]], writes=[tmp[name]], pw=True)
    if part in ("all", "transposes"):
        for step in build_pwc_steps(p, t):
            step()


def build_pwc_steps(p, t):
    nc = p.nc
    emit_pw_cast(p, t, part="none")
    tmp = t["_pwc_tmp"]
    if "_pwc_stg" not in t:
        t["_pwc_stack"] = ExitStack()
        t["_pwc_stg"] = [Buf(t["_pwc_stack"].enter_context(
            nc.sbuf_tensor(f"pwc_s{i}", [128, H1], BF16)), f"pwc_s{i}")
            for i in range(3)]
    stg = t["_pwc_stg"]
    steps = []
    si = [0]
    for name, out_name, rows, cols, cw in [("w1_w", "w1D", H1, C, 512),
                                           ("w2_w", "w2D", H1, C, 512),
                                           ("h1_w", "h1D", H1, H1, 256),
                                           ("h2_w", "h2D", H1, H1, 256),
                                           ("w3_w", "w3D", C, H1, 1024)]:
        for ci in range(cols // 128):
            def step(name=name, out_name=out_name, rows=rows, ci=ci, cw=cw):
                sbuf = stg[si[0] % len(stg)]
                si[0] += 1
                p.dma("sync", sbuf[:, 0:rows], tmp[name].ap()[:, ci * 128:(ci + 1) * 128],
                      reads=[tmp[name]], writes=[sbuf], transpose=True)
                if out_name == "w3D":
                    dst = t[out_name].ap()[:, ci, :]
                    src = sbuf[:, 0:rows]
                else:
                    dst = t[out_name].ap()[:, :, ci, :]
                    src = sbuf.ap()[:, 0:rows].rearrange("p (w c) -> p w c", c=cw)
                p.dma("scalar", dst, src, reads=[sbuf], writes=[t[out_name]], pw=True)
            steps.append(step)
    return steps




def emit_p1(p, t, st, layer, feeder=None):
    nc = p.nc
    if layer == 0:
        xin, wname, bname = t["x0"], "w_attn", "b_attn"
        qf, kf, vt = t["qf0"], t["kf0"], t["vt0"]
    else:
        xin, wname, bname = t["x1"], "qkv_w", "qkv_b"
        qf, kf, vt = t["qf1"], t["kf1"], t["vt1"]

    with ExitStack() as sx:
        E = sx.enter_context
        w_sb = Buf(E(nc.sbuf_tensor(f"p1{layer}_w", [128, 8 * 3072], BF16)), "p1_w")
        bias = Buf(E(nc.sbuf_tensor(f"p1{layer}_b", [128, 24], F32)), "p1_b")
        bvrep = Buf(E(nc.sbuf_tensor(f"p1{layer}_bvrep", [128, 1024], F32)), "p1_bvrep")
        ones1 = Buf(E(nc.sbuf_tensor(f"p1{layer}_ones", [1, 128], BF16)), "p1_ones")
        brow = Buf(E(nc.sbuf_tensor(f"p1{layer}_brow", [1, 1024], BF16)), "p1_brow")
        browf = Buf(E(nc.sbuf_tensor(f"p1{layer}_browf", [1, 1024], F32)), "p1_browf")
        xt = [Buf(E(nc.sbuf_tensor(f"p1{layer}_x{i}", [128, 1024], F32)), f"p1_x{i}") for i in range(8)]
        mu = Buf(E(nc.sbuf_tensor(f"p1{layer}_mu", [128, 32], F32)), "p1_mu")
        rs = Buf(E(nc.sbuf_tensor(f"p1{layer}_rs", [128, 32], F32)), "p1_rs")
        sq = Buf(E(nc.sbuf_tensor(f"p1{layer}_sq", [128, 32], F32)), "p1_sq")
        var = Buf(E(nc.sbuf_tensor(f"p1{layer}_var", [128, 32], F32)), "p1_var")
        sqjunk = Buf(E(nc.sbuf_tensor(f"p1{layer}_sqjunk", [128, 1024], F32)), "p1_sqjunk")
        lnjunk = Buf(E(nc.sbuf_tensor(f"p1{layer}_lnjunk", [128, 1], F32)), "p1_lnjunk")
        xjunk = Buf(E(nc.sbuf_tensor(f"p1{layer}_xjunk", [128, 1024], F32)), "p1_xjunk")
        xnb = [Buf(E(nc.sbuf_tensor(f"p1{layer}_xnb{i}", [128, 1024], BF16)), f"p1_xnb{i}") for i in range(2)]
        xnf = [Buf(E(nc.sbuf_tensor(f"p1{layer}_xnf{i}", [128, 4096], BF16)), f"p1_xnf{i}") for i in range(2)]
        ev = [Buf(E(nc.sbuf_tensor(f"p1{layer}_ev{i}", [128, 512], BF16)), f"p1_ev{i}") for i in range(4)]
        vev = [Buf(E(nc.sbuf_tensor(f"p1{layer}_vev{i}", [128, 1024], BF16)), f"p1_vev{i}") for i in range(2)]
        sq2 = Buf(E(nc.sbuf_tensor(f"p1{layer}_sq2", [128, 512], F32)), "p1_sq2")
        hsel = Buf(E(nc.sbuf_tensor(f"p1{layer}_hsel", [128, 3], F32)), "p1_hsel")
        nrmq = Buf(E(nc.sbuf_tensor(f"p1{layer}_nrmq", [1, 32], F32)), "p1_nrmq")
        nrmk = Buf(E(nc.sbuf_tensor(f"p1{layer}_nrmk", [2, 8], F32)), "p1_nrmk")
        kmt = Buf(E(nc.sbuf_tensor(f"p1{layer}_kmt", [2, 2], F32)), "p1_kmt")
        pst = [Buf(E(nc.psum_tensor(f"p1{layer}_pst{i}", [128, 1024], BF16)), f"p1_pst{i}") for i in range(2)]
        psq = [Buf(E(nc.psum_tensor(f"p1{layer}_psq{i}", [128, 512], F32)), f"p1_psq{i}") for i in range(4)]
        psv = [Buf(E(nc.psum_tensor(f"p1{layer}_psv{i}", [128, 512], F32)), f"p1_psv{i}") for i in range(2)]

        for cc in range(8):
            p.dma("gpsimd", w_sb[:, cc * 3072:(cc + 1) * 3072],
                  t[wname].ap()[cc * 128:(cc + 1) * 128, :],
                  reads=[t[wname]], writes=[w_sb], pw=True)
        p.dma("sync", bias[:], t[bname].ap().rearrange("(n p) -> p n", p=128),
              reads=[t[bname]], writes=[bias], allow_slow_non_contiguous=True)
        p.do("gpsimd", lambda g: g.memset(ones1[:], 1.0), writes=[ones1])
        p.do("gpsimd", lambda g: g.memset(hsel[:], 0.0), writes=[hsel])
        p.do("gpsimd", lambda g: g.memset(hsel[:, 0:1], 1.0), writes=[hsel])
        p.do("gpsimd", lambda g: g.memset(hsel.ap()[0:64, 1:2], 1.0), writes=[hsel])
        p.do("gpsimd", lambda g: g.memset(hsel.ap()[64:128, 2:3], 1.0), writes=[hsel])
        p.dma("scalar", browf[0:1, :], t[bname].ap()[2048:3072].rearrange("n -> () n"),
              reads=[t[bname]], writes=[browf], fence=True)
        p.do("vector", lambda v: v.tensor_copy(out=brow[0:1, :], in_=browf[0:1, :]),
             reads=[browf], writes=[brow])
        for half in range(2):
            p.do("tensor", lambda tt, half=half: tt.matmul(
                psq[0][:, :], ones1[0:1, :], brow[0:1, half * 512:(half + 1) * 512],
                start=True, stop=True), reads=[ones1, brow], writes=[psq[0]])
            p.do("vector", lambda v, half=half: v.tensor_copy(
                out=bvrep[:, half * 512:(half + 1) * 512], in_=psq[0][:, :]),
                 reads=[psq[0]], writes=[bvrep])

        nblk = T // 512
        for bi in range(nblk):
            xnf_b = xnf[bi % 2]
            for j in range(4):
                ti = bi * 4 + j
                cl = slice(ti % 32, ti % 32 + 1)
                xtile = xt[ti % 4]
                p.dma("sync", xtile[:], xin.ap()[ti * 128:(ti + 1) * 128, :],
                      reads=[xin], writes=[xtile])
                p.do("vector", lambda v, xtile=xtile: v.tensor_tensor(
                    out=sqjunk[:], in0=xtile[:], in1=xtile[:], op=ALU.mult),
                     reads=[xtile], writes=[sqjunk])
                p.do("vector", lambda v, cl=cl: v.tensor_reduce(
                    out=sq[:, cl], in_=sqjunk[:], axis=AX.X, op=ALU.add),
                     reads=[sqjunk], writes=[sq])
                p.do("vector", lambda v, xtile=xtile, cl=cl: v.tensor_reduce(
                    out=mu[:, cl], in_=xtile[:], axis=AX.X, op=ALU.add),
                     reads=[xtile], writes=[mu])
                p.do("vector", lambda v, cl=cl: v.tensor_scalar(
                    out=mu[:, cl], in0=mu[:, cl], scalar1=1.0 / C, scalar2=None,
                    op0=ALU.mult), reads=[mu], writes=[mu])
                p.do("vector", lambda v, cl=cl: v.scalar_tensor_tensor(
                    out=var[:, cl], in0=mu[:, cl], scalar=1.0, in1=mu[:, cl],
                    op0=ALU.mult, op1=ALU.mult), reads=[mu], writes=[var])
                p.do("vector", lambda v, cl=cl: v.tensor_scalar(
                    out=lnjunk[:], in0=sq[:, cl], scalar1=1.0 / C, scalar2=1.0e-5,
                    op0=ALU.mult, op1=ALU.add), reads=[sq], writes=[lnjunk])
                p.do("vector", lambda v, cl=cl: v.tensor_tensor(
                    out=var[:, cl], in0=lnjunk[:], in1=var[:, cl], op=ALU.subtract),
                     reads=[lnjunk, var], writes=[var])
                p.do("scalar", lambda s, cl=cl: s.activation(
                    out=lnjunk[:], in_=var[:, cl], func=AF.Ln),
                     reads=[var], writes=[lnjunk])
                p.do("scalar", lambda s, cl=cl: s.activation(
                    out=rs[:, cl], in_=lnjunk[:], func=AF.Exp, scale=-0.5),
                     reads=[lnjunk], writes=[rs])
                xnb_t = xnb[ti % 2]
                mb = bcast0(mu.ap().rearrange("p n -> p n ()")[:, cl], 1024)
                rb = bcast0(rs.ap().rearrange("p n -> p n ()")[:, cl], 1024)
                p.do("vector", lambda v, xtile=xtile, mb=mb: v.tensor_tensor(
                    out=xjunk.ap().rearrange("p (a f) -> p a f", a=1),
                    in0=xtile.ap().rearrange("p (a f) -> p a f", a=1),
                    in1=mb, op=ALU.subtract), reads=[xtile, mu], writes=[xjunk])
                p.do("vector", lambda v, rb=rb, xnb_t=xnb_t: v.tensor_tensor(
                    out=xnb_t.ap().rearrange("p (a f) -> p a f", a=1),
                    in0=xjunk.ap().rearrange("p (a f) -> p a f", a=1),
                    in1=rb, op=ALU.mult), reads=[xjunk, rs], writes=[xnb_t])
                pstb = pst[ti % 2]
                def ftp(tt, xnb_t=xnb_t, pstb=pstb):
                    ii = None
                    for cc in range(8):
                        ii = tt.transpose(pstb[:, cc * 128:(cc + 1) * 128],
                                          xnb_t[:, cc * 128:(cc + 1) * 128], st.ident[:])
                    return ii
                p.do("tensor", ftp, reads=[xnb_t, st.ident], writes=[pstb])
                def fev(v, pstb=pstb, xnf_b=xnf_b, j=j):
                    out = xnf_b.ap().rearrange("p (c t) -> p c t", c=8)[:, :, j * 128:(j + 1) * 128]
                    return v.tensor_copy(out=out,
                                         in_=pstb.ap().rearrange("p (c t) -> p c t", c=8))
                p.do("vector", fev, reads=[pstb], writes=[xnf_b])
                if feeder is not None:
                    feeder(bi * 4 + j)
            if layer == 1:
                p.dma("scalar", t["xn1f"].ap()[bi], xnf_b[:],
                      reads=[xnf_b], writes=[t["xn1f"]], pw=True)
            for nc_i in range(16):
                bank = psq[nc_i % 3]
                def fmm(tt, nc_i=nc_i, bank=bank, xnf_b=xnf_b):
                    ii = None
                    for cc in range(8):
                        ii = tt.matmul(bank[:, :],
                                       w_sb[:, cc * 3072 + nc_i * 128:cc * 3072 + (nc_i + 1) * 128],
                                       xnf_b[:, cc * 512:(cc + 1) * 512],
                                       start=(cc == 0), stop=(cc == 7))
                    return ii
                p.do("tensor", fmm, reads=[w_sb, xnf_b], writes=[bank])
                evt = ev[nc_i % 4]
                p.do("vector", lambda v, bank=bank, evt=evt, nc_i=nc_i: v.tensor_scalar(
                    out=evt[:], in0=bank[:], scalar1=bias[:, nc_i:nc_i + 1],
                    scalar2=None, op0=ALU.add),
                     reads=[bank], sreads=[bias], writes=[evt])
                dst = qf if nc_i < 8 else kf
                row = (nc_i % 8) * 128
                p.dma("sync", dst.ap()[row:row + 128, bi * 512:(bi + 1) * 512], evt[:],
                      reads=[evt], writes=[dst], pw=True)
                ps_n = psq[3]
                p.do("vector", lambda v, evt=evt: v.tensor_tensor(
                    out=sq2[:], in0=evt[:], in1=evt[:], op=ALU.mult),
                     reads=[evt], writes=[sq2])
                if nc_i < 8:
                    p.do("tensor", lambda tt, nc_i=nc_i, ps_n=ps_n: tt.matmul(
                        ps_n.ap()[0:1, :], hsel[:, 0:1], sq2[:],
                        start=(nc_i == 0), stop=(nc_i == 7)),
                         reads=[hsel, sq2], writes=[ps_n])
                    if nc_i == 7:
                        p.do("vector", lambda v, bi=bi, ps_n=ps_n: v.tensor_reduce(
                            out=nrmq.ap()[0:1, bi * 4:(bi + 1) * 4],
                            in_=ps_n.ap()[0:1, :].rearrange("p (g c) -> p g c", g=4),
                            axis=AX.X, op=ALU.max), reads=[ps_n], writes=[nrmq])
                elif layer == 0:
                    p.do("tensor", lambda tt, nc_i=nc_i, ps_n=ps_n: tt.matmul(
                        ps_n.ap()[0:1, :], hsel[:, 0:1], sq2[:],
                        start=(nc_i == 8), stop=(nc_i == 15)),
                         reads=[hsel, sq2], writes=[ps_n])
                    if nc_i == 15:
                        p.do("vector", lambda v, bi=bi, ps_n=ps_n: v.tensor_reduce(
                            out=nrmk.ap()[0:1, bi:bi + 1], in_=ps_n.ap()[0:1, :],
                            axis=AX.X, op=ALU.max), reads=[ps_n], writes=[nrmk])
                else:
                    hc = nc_i - 8
                    p.do("tensor", lambda tt, ps_n=ps_n: tt.matmul(
                        ps_n.ap()[0:2, :], hsel[:, 1:3], sq2[:],
                        start=True, stop=True), reads=[hsel, sq2], writes=[ps_n])
                    if bi == 0:
                        p.do("vector", lambda v, hc=hc, ps_n=ps_n: v.tensor_reduce(
                            out=nrmk.ap()[0:2, hc:hc + 1], in_=ps_n.ap()[0:2, :],
                            axis=AX.X, op=ALU.max), reads=[ps_n], writes=[nrmk])
                    else:
                        p.do("vector", lambda v, ps_n=ps_n: v.tensor_reduce(
                            out=kmt.ap()[0:2, 0:1], in_=ps_n.ap()[0:2, :],
                            axis=AX.X, op=ALU.max), reads=[ps_n], writes=[kmt])
                        p.do("vector", lambda v, hc=hc: v.tensor_tensor(
                            out=nrmk.ap()[0:2, hc:hc + 1], in0=nrmk.ap()[0:2, hc:hc + 1],
                            in1=kmt.ap()[0:2, 0:1], op=ALU.max),
                             reads=[nrmk, kmt], writes=[nrmk])
            for j in range(4):
                for half in range(2):
                    bank = psv[half]
                    def fmmv(tt, j=j, half=half, bank=bank, xnf_b=xnf_b):
                        ii = None
                        for cc in range(8):
                            ii = tt.matmul(bank[:, :],
                                           xnf_b[:, cc * 512 + j * 128:cc * 512 + (j + 1) * 128],
                                           w_sb[:, cc * 3072 + 2048 + half * 512:cc * 3072 + 2048 + (half + 1) * 512],
                                           start=(cc == 0), stop=(cc == 7))
                        return ii
                    p.do("tensor", fmmv, reads=[xnf_b, w_sb], writes=[bank])
                vv = vev[j % 2]
                def fvev(v, vv=vv):
                    v.tensor_tensor(out=vv[:, 0:512], in0=psv[0][:], in1=bvrep[:, 0:512],
                                    op=ALU.add)
                    return v.tensor_tensor(out=vv[:, 512:1024], in0=psv[1][:],
                                           in1=bvrep[:, 512:1024], op=ALU.add)
                p.do("vector", fvev, reads=[psv[0], psv[1], bvrep], writes=[vv])
                row = bi * 512 + j * 128
                p.dma("sync", vt.ap()[row:row + 128, :], vv[:], reads=[vv], writes=[vt], pw=True)
        qn_d = t["qn0"] if layer == 0 else t["qn1"]
        p.do("scalar", lambda s: s.activation(out=nrmq[:], in_=nrmq[:], func=AF.Sqrt),
             reads=[nrmq], writes=[nrmq])
        p.dma("sync", qn_d.ap()[0:1, 0:32], nrmq[:], reads=[nrmq], writes=[qn_d], pw=True)
        if layer == 0:
            p.do("vector", lambda v: v.tensor_reduce(
                out=kmt.ap()[0:1, 0:1], in_=nrmk.ap()[0:1, 0:8], axis=AX.X,
                op=ALU.max), reads=[nrmk], writes=[kmt])
            p.do("scalar", lambda s: s.activation(out=kmt.ap()[0:1, 0:1],
                                                  in_=kmt.ap()[0:1, 0:1], func=AF.Sqrt),
                 reads=[kmt], writes=[kmt])
            p.dma("sync", qn_d.ap()[0:1, 32:33], kmt.ap()[0:1, 0:1],
                  reads=[kmt], writes=[qn_d], pw=True)
        else:
            p.do("scalar", lambda s: s.activation(out=nrmk[:], in_=nrmk[:], func=AF.Sqrt),
                 reads=[nrmk], writes=[nrmk])
            p.dma("sync", qn_d.ap()[0:2, 32:40], nrmk[:], reads=[nrmk], writes=[qn_d], pw=True)
        if layer == 0:
            p.dma("sync", t["muR0"].ap()[:, 0:32], mu[:], reads=[mu], writes=[t["muR0"]], pw=True)
            p.dma("sync", t["muR0"].ap()[:, 32:64], rs[:], reads=[rs], writes=[t["muR0"]], pw=True)




def emit_p3(p, t, st):
    nc = p.nc
    NCH = [(i * 512, min(512, I0 - i * 512)) for i in range((I0 + 511) // 512)]
    with ExitStack() as sx:
        E = sx.enter_context
        wqg = Buf(E(nc.sbuf_tensor("p3_wqg", [128, 8 * I0], BF16)), "p3_wqg")
        wqu = Buf(E(nc.sbuf_tensor("p3_wqu", [128, 8 * I0], BF16)), "p3_wqu")
        wqd = Buf(E(nc.sbuf_tensor("p3_wqd", [128, 22 * 1024], BF16)), "p3_wqd")
        mur = Buf(E(nc.sbuf_tensor("p3_mur", [128, 64], F32)), "p3_mur")
        xt = [Buf(E(nc.sbuf_tensor(f"p3_x0t{i}", [128, 1024], F32)), f"p3_x0t{i}") for i in range(2)]
        xat = [Buf(E(nc.sbuf_tensor(f"p3_xa0{i}", [128, 1024], F32)), f"p3_xa0{i}") for i in range(2)]
        xn32 = Buf(E(nc.sbuf_tensor("p3_xn32", [128, 1024], F32)), "p3_xn32")
        xq = Buf(E(nc.sbuf_tensor("p3_xq", [128, 1024], BF16)), "p3_xq")
        xqf = Buf(E(nc.sbuf_tensor("p3_xqf", [128, 1024], BF16)), "p3_xqf")
        h_sb = Buf(E(nc.sbuf_tensor("p3_h", [128, I0], F32)), "p3_h")
        hq = Buf(E(nc.sbuf_tensor("p3_hq", [128, I0], BF16)), "p3_hq")
        hqf = Buf(E(nc.sbuf_tensor("p3_hqf", [128, I0], BF16)), "p3_hqf")
        gs_sb = Buf(E(nc.sbuf_tensor("p3_gs", [128, 512], F32)), "p3_gs")
        x1t = Buf(E(nc.sbuf_tensor("p3_x1t", [128, 1024], F32)), "p3_x1t")
        q = {}
        for nm, w in [("bm", 88), ("sch", 88), ("A", 1408), ("t1", 1408),
                      ("t2", 1408)]:
            q[nm] = Buf(E(nc.sbuf_tensor(f"p3q_{nm}", [128, w], F32)), nm)
        ps_g = Buf(E(nc.psum_tensor("p3_psg", [128, 512], F32)), "p3_psg")
        ps_u = Buf(E(nc.psum_tensor("p3_psu", [128, 512], F32)), "p3_psu")
        ps_t = [Buf(E(nc.psum_tensor(f"p3_pst{i}", [128, 1024], BF16)), f"p3_pst{i}") for i in range(2)]
        ps_d = [Buf(E(nc.psum_tensor(f"p3_psd{i}", [128, 512], F32)), f"p3_psd{i}") for i in range(2)]

        for cc in range(8):
            p.dma("sync", wqg[:, cc * I0:(cc + 1) * I0],
                  t["wqg"].ap()[:, cc * 128:(cc + 1) * 128],
                  reads=[t["wqg"]], writes=[wqg], transpose=True, pw=True)
            p.dma("sync", wqu[:, cc * I0:(cc + 1) * I0],
                  t["wqu"].ap()[:, cc * 128:(cc + 1) * 128],
                  reads=[t["wqu"]], writes=[wqu], transpose=True, pw=True)
        for ic in range(22):
            p.dma("sync", wqd[:, ic * 1024:(ic + 1) * 1024],
                  t["wqd"].ap()[:, ic * 128:(ic + 1) * 128],
                  reads=[t["wqd"]], writes=[wqd], transpose=True, pw=True)
        p.dma("sync", mur[:], t["muR0"].ap(), reads=[t["muR0"]], writes=[mur])

        for ti in range(32):
            xtile, xat_t = xt[ti % 2], xat[ti % 2]
            p.dma("sync", xtile[:], t["x0"].ap()[ti * 128:(ti + 1) * 128, :],
                  reads=[t["x0"]], writes=[xtile])
            p.dma("scalar", xat_t[:], t["xa"].ap()[ti * 128:(ti + 1) * 128, :],
                  reads=[t["xa"]], writes=[xat_t])
            mb = bcast0(mur.ap().rearrange("p n -> p n ()")[:, ti:ti + 1], 1024)
            rb = bcast0(mur.ap().rearrange("p n -> p n ()")[:, 32 + ti:32 + ti + 1], 1024)
            p.do("vector", lambda v, xtile=xtile, mb=mb: v.tensor_tensor(
                out=xn32.ap().rearrange("p (a f) -> p a f", a=1),
                in0=xtile.ap().rearrange("p (a f) -> p a f", a=1),
                in1=mb, op=ALU.subtract), reads=[xtile, mur], writes=[xn32])
            p.do("vector", lambda v, rb=rb: v.tensor_tensor(
                out=xn32.ap().rearrange("p (a f) -> p a f", a=1),
                in0=xn32.ap().rearrange("p (a f) -> p a f", a=1),
                in1=rb, op=ALU.mult), reads=[xn32, mur], writes=[xn32])
            emit_quant(p, xn32, xq, q, 1024, eng="vector", sgn_eng="scalar")
            if DBG_QT is not None and ti == DBG_QT:
                p.dma("sync", t["dbg_p"].ap()[:, 0:1024], xq[:], reads=[xq], writes=[t["dbg_p"]])
                p.dma("sync", t["dbg_s"].ap()[:, 0:1024], xn32[:], reads=[xn32], writes=[t["dbg_s"]])
            pst = ps_t[0]
            def ftp(tt, pst=pst):
                ii = None
                for cc in range(8):
                    ii = tt.transpose(pst[:, cc * 128:(cc + 1) * 128],
                                      xq[:, cc * 128:(cc + 1) * 128], st.ident[:])
                return ii
            p.do("tensor", ftp, reads=[xq, st.ident], writes=[pst])
            p.do("vector", lambda v, pst=pst: v.tensor_copy(
                out=xqf[:], in_=pst[:]), reads=[pst], writes=[xqf])
            for n0, nw in NCH:
                def fg(tt, n0=n0, nw=nw):
                    ii = None
                    for cc in range(8):
                        ii = tt.matmul(ps_g[:, 0:nw], xqf[:, cc * 128:(cc + 1) * 128],
                                       wqg[:, cc * I0 + n0:cc * I0 + n0 + nw],
                                       start=(cc == 0), stop=(cc == 7))
                    return ii
                p.do("tensor", fg, reads=[xqf, wqg], writes=[ps_g])
                def fu(tt, n0=n0, nw=nw):
                    ii = None
                    for cc in range(8):
                        ii = tt.matmul(ps_u[:, 0:nw], xqf[:, cc * 128:(cc + 1) * 128],
                                       wqu[:, cc * I0 + n0:cc * I0 + n0 + nw],
                                       start=(cc == 0), stop=(cc == 7))
                    return ii
                p.do("tensor", fu, reads=[xqf, wqu], writes=[ps_u])
                p.do("scalar", lambda s, nw=nw: s.activation(
                    out=gs_sb[:, 0:nw], in_=ps_g[:, 0:nw], func=AF.Silu),
                     reads=[ps_g], writes=[gs_sb])
                p.do("vector", lambda v, n0=n0, nw=nw: v.tensor_tensor(
                    out=h_sb[:, n0:n0 + nw], in0=ps_u[:, 0:nw], in1=gs_sb[:, 0:nw],
                    op=ALU.mult), reads=[ps_u, gs_sb], writes=[h_sb])
            if DBG_QT is not None and ti == DBG_QT:
                p.dma("sync", t["dbg_p"].ap()[:, 1024:3840], wqg[:, 0:2816], reads=[wqg], writes=[t["dbg_p"]])
                p.dma("sync", t["dbg_stat"].ap()[:, 0:64], mur[:], reads=[mur], writes=[t["dbg_stat"]])
                p.dma("sync", t["dbg_at"].ap()[:, 0:1024], xqf[:], reads=[xqf], writes=[t["dbg_at"]])
                p.dma("gpsimd", t["dbg_pt"].ap()[:, 0:2816], h_sb[:], reads=[h_sb], writes=[t["dbg_pt"]])
            emit_quant(p, h_sb, hq, q, 1408, eng="vector", sgn_eng="scalar", c0=0)
            emit_quant(p, h_sb, hq, q, 1408, eng="vector", sgn_eng="scalar", c0=1408)
            if DBG_QT is not None and ti == DBG_QT:
                p.dma("sync", t["dbg_atf"].ap()[:, 0:1024], hq[:, 0:1024], reads=[hq], writes=[t["dbg_atf"]])
            for g0 in range(0, 22, 8):
                gn = min(8, 22 - g0)
                pst = ps_t[(g0 // 8) % 2]
                def fth(tt, g0=g0, gn=gn, pst=pst):
                    ii = None
                    for k in range(gn):
                        ii = tt.transpose(pst[:, k * 128:(k + 1) * 128],
                                          hq[:, (g0 + k) * 128:(g0 + k + 1) * 128],
                                          st.ident[:])
                    return ii
                p.do("tensor", fth, reads=[hq, st.ident], writes=[pst])
                p.do("vector", lambda v, g0=g0, gn=gn, pst=pst: v.tensor_copy(
                    out=hqf[:, g0 * 128:(g0 + gn) * 128], in_=pst[:, 0:gn * 128]),
                     reads=[pst], writes=[hqf])
            for half in range(2):
                bank = ps_d[half]
                def fd(tt, half=half, bank=bank):
                    ii = None
                    for ic in range(22):
                        ii = tt.matmul(bank[:, :], hqf[:, ic * 128:(ic + 1) * 128],
                                       wqd[:, ic * 1024 + half * 512:ic * 1024 + (half + 1) * 512],
                                       start=(ic == 0), stop=(ic == 21))
                    return ii
                p.do("tensor", fd, reads=[hqf, wqd], writes=[bank])
                p.do("vector", lambda v, half=half, bank=bank, xat_t=xat_t: v.tensor_tensor(
                    out=x1t[:, half * 512:(half + 1) * 512], in0=bank[:],
                    in1=xat_t[:, half * 512:(half + 1) * 512], op=ALU.add),
                     reads=[bank, xat_t], writes=[x1t])
            p.dma("sync", t["x1"].ap()[ti * 128:(ti + 1) * 128, :], x1t[:],
                  reads=[x1t], writes=[t["x1"]], pw=True)




def emit_p6(p, t, st):
    nc = p.nc
    TB = 1024
    NB = T // TB
    with ExitStack() as sx:
        E = sx.enter_context
        xnf = Buf(E(nc.sbuf_tensor("p6_xnf", [128, 8 * TB], BF16)), "p6_xnf")
        u_sb = Buf(E(nc.sbuf_tensor("p6_u", [128, 32 * TB], BF16)), "p6_u")
        v_sb = Buf(E(nc.sbuf_tensor("p6_v", [128, 32 * TB], BF16)), "p6_v")
        hw_sb = [Buf(E(nc.sbuf_tensor(f"p6_hw{i}", [128, 8192], BF16)), f"p6_hw{i}")
                 for i in range(2)]
        s_sb = Buf(E(nc.sbuf_tensor("p6_s", [128, 4 * 32], F32)), "p6_s")
        s3rep = Buf(E(nc.sbuf_tensor("p6_s3rep", [128, 1024], BF16)), "p6_s3rep")
        gs_sb = Buf(E(nc.sbuf_tensor("p6_gs", [128, 512], F32)), "p6_gs")
        xbt = Buf(E(nc.sbuf_tensor("p6_xbt", [128, 1024], F32)), "p6_xbt")
        outt = Buf(E(nc.sbuf_tensor("p6_outt", [128, 1024], F32)), "p6_outt")
        ps_1 = [Buf(E(nc.psum_tensor(f"p6_ps1{i}", [128, 512], F32)), f"p6_ps1{i}") for i in range(2)]
        ps_2 = [Buf(E(nc.psum_tensor(f"p6_ps2{i}", [128, 512], F32)), f"p6_ps2{i}") for i in range(2)]
        ps_v = [Buf(E(nc.psum_tensor(f"p6_psv{i}", [128, 512], F32)), f"p6_psv{i}") for i in range(2)]
        ps_d = [Buf(E(nc.psum_tensor(f"p6_psd{i}", [128, 512], F32)), f"p6_psd{i}") for i in range(2)]

        for i, nm in enumerate(["w1_s", "w2_s", "h1_s", "h2_s"]):
            p.dma("sync", s_sb[:, i * 32:(i + 1) * 32],
                  t[nm].ap().rearrange("(n p) -> p n", p=128),
                  reads=[t[nm]], writes=[s_sb], allow_slow_non_contiguous=True)
        with ExitStack() as bsx:
            ones1 = Buf(bsx.enter_context(nc.sbuf_tensor("p6_ones", [1, 128], BF16)), "p6_ones")
            browf = Buf(bsx.enter_context(nc.sbuf_tensor("p6_browf", [1, 1024], F32)), "p6_browf")
            browbf = Buf(bsx.enter_context(nc.sbuf_tensor("p6_browb", [1, 1024], BF16)), "p6_browb")
            s3f = Buf(bsx.enter_context(nc.sbuf_tensor("p6_s3f", [128, 1024], F32)), "p6_s3f")
            p.do("gpsimd", lambda g: g.memset(ones1[:], 1.0), writes=[ones1])
            emit_bcast_row(p, t["w3_s"].ap(), s3f, ps_d[0], ones1, browf, browbf, C)
            p.do("vector", lambda v: v.tensor_copy(out=s3rep[:], in_=s3f[:]),
                 reads=[s3f], writes=[s3rep])
            p.phase_barrier()

        QROT = P6_QROT
        qi = [0]

        def qn():
            q = QROT[qi[0] % len(QROT)]
            qi[0] += 1
            return q

        for bi in range(NB):
            for hb in range(2):
                p.dma(qn(),
                      xnf.ap().rearrange("p (cc tb) -> p cc tb", tb=TB)[:, :, hb * 512:(hb + 1) * 512],
                      t["xn1f"].ap()[bi * 2 + hb].rearrange("p (cc tb) -> p cc tb", tb=512),
                      reads=[t["xn1f"]], writes=[xnf], pw=True)
            for hwi in range(8):
                wb = hw_sb[hwi % 2]
                p.dma(qn(), wb[:, 0:4096],
                      t["w1D"].ap()[:, hwi].rearrange("p cc c -> p (cc c)"),
                      reads=[t["w1D"]], writes=[wb], pw=True)
                p.dma(qn(), wb[:, 4096:8192],
                      t["w2D"].ap()[:, hwi].rearrange("p cc c -> p (cc c)"),
                      reads=[t["w2D"]], writes=[wb], pw=True)
                for hch in range(4):
                    h = hwi * 4 + hch
                    for th in range(2):
                        b1, b2 = ps_1[th], ps_2[th]
                        def f1(tt, hch=hch, b1=b1, wb=wb, th=th):
                            ii = None
                            for cc in range(8):
                                ii = tt.matmul(
                                    b1[:, :],
                                    wb[:, cc * 512 + hch * 128:cc * 512 + (hch + 1) * 128],
                                    xnf[:, cc * TB + th * 512:cc * TB + (th + 1) * 512],
                                    start=(cc == 0), stop=(cc == 7))
                            return ii
                        p.do("tensor", f1, reads=[wb, xnf], writes=[b1])
                        def f2(tt, hch=hch, b2=b2, wb=wb, th=th):
                            ii = None
                            for cc in range(8):
                                ii = tt.matmul(
                                    b2[:, :],
                                    wb[:, 4096 + cc * 512 + hch * 128:4096 + cc * 512 + (hch + 1) * 128],
                                    xnf[:, cc * TB + th * 512:cc * TB + (th + 1) * 512],
                                    start=(cc == 0), stop=(cc == 7))
                            return ii
                        p.do("tensor", f2, reads=[wb, xnf], writes=[b2])
                        p.do("scalar", lambda s, h=h, b1=b1: s.activation(
                            out=gs_sb[:, 0:512], in_=b1[:], func=AF.Silu,
                            scale=s_sb[:, h:h + 1]), reads=[b1], sreads=[s_sb],
                             writes=[gs_sb])
                        p.do("vector", lambda v, h=h, th=th, b2=b2: v.scalar_tensor_tensor(
                            out=u_sb[:, h * TB + th * 512:h * TB + (th + 1) * 512], in0=b2[:],
                            scalar=s_sb[:, 32 + h:32 + h + 1], in1=gs_sb[:, 0:512],
                            op0=ALU.mult, op1=ALU.mult), reads=[b2, gs_sb],
                             sreads=[s_sb], writes=[u_sb])
            for stage, (wt_name, src_sb, dst_sb, scol) in enumerate(
                    [("h1D", u_sb, v_sb, 64), ("h2D", v_sb, u_sb, 96)]):
                for how in range(16):
                    hw = hw_sb[how % 2]
                    for g0 in range(0, 32, 16):
                        p.dma(qn(),
                              hw.ap()[:, g0 * 256:(g0 + 16) * 256],
                              t[wt_name].ap()[:, how, g0:g0 + 16, :].rearrange(
                                  "p h c -> p (h c)"),
                              reads=[t[wt_name]], writes=[hw], pw=True)
                    for hoch in range(2):
                        for th in range(2):
                            ho = how * 2 + hoch
                            bank = ps_v[th]
                            def fv(tt, hw=hw, hoch=hoch, th=th, bank=bank, src_sb=src_sb):
                                ii = None
                                for hc in range(32):
                                    ii = tt.matmul(
                                        bank[:, :],
                                        hw[:, hc * 256 + hoch * 128:hc * 256 + (hoch + 1) * 128],
                                        src_sb[:, hc * TB + th * 512:hc * TB + (th + 1) * 512],
                                        start=(hc == 0), stop=(hc == 31))
                                return ii
                            p.do("tensor", fv, reads=[hw, src_sb], writes=[bank])
                            p.do("scalar", lambda s, ho=ho, th=th, bank=bank, dst_sb=dst_sb,
                                 scol=scol: s.activation(
                                     out=dst_sb[:, ho * TB + th * 512:ho * TB + (th + 1) * 512],
                                     in_=bank[:],
                                     func=AF.Gelu, scale=s_sb[:, scol + ho:scol + ho + 1]),
                                 reads=[bank], sreads=[s_sb], writes=[dst_sb])
            for half in range(2):
                for b2 in range(2):
                    p.dma(qn(),
                          hw_sb[b2].ap()[:, 0:8192].rearrange("p (h c) -> p h c", c=512),
                          t["w3D"].ap()[:, b2 * 16:(b2 + 1) * 16,
                                        half * 512:(half + 1) * 512],
                          reads=[t["w3D"]], writes=[hw_sb[b2]], pw=True)
                for tch in range(8):
                    row = bi * TB + tch * 128
                    xb_t = xbt
                    p.dma("scalar", xb_t[:, half * 512:(half + 1) * 512],
                          t["xb"].ap()[row:row + 128, half * 512:(half + 1) * 512],
                          reads=[t["xb"]], writes=[xb_t], pw=True)
                    bank = ps_d[tch % 2]
                    def fd(tt, tch=tch, bank=bank):
                        ii = None
                        for hc in range(32):
                            w3t = hw_sb[hc // 16]
                            ii = tt.matmul(
                                bank[:, :],
                                u_sb[:, hc * TB + tch * 128:hc * TB + (tch + 1) * 128],
                                w3t[:, (hc % 16) * 512:(hc % 16 + 1) * 512],
                                start=(hc == 0), stop=(hc == 31))
                        return ii
                    p.do("tensor", fd, reads=[u_sb, hw_sb[0], hw_sb[1]], writes=[bank])
                    ot = outt
                    p.do("vector", lambda v, half=half, bank=bank, ot=ot: v.tensor_tensor(
                        out=ot[:, half * 512:(half + 1) * 512], in0=bank[:],
                        in1=s3rep[:, half * 512:(half + 1) * 512], op=ALU.mult),
                         reads=[bank, s3rep], writes=[ot])
                    p.do("vector", lambda v, half=half, ot=ot, xb_t=xb_t: v.tensor_tensor(
                        out=ot[:, half * 512:(half + 1) * 512],
                        in0=ot[:, half * 512:(half + 1) * 512],
                        in1=xb_t[:, half * 512:(half + 1) * 512], op=ALU.add),
                         reads=[ot, xb_t], writes=[ot])
                    p.dma("sync", t["out"].ap()[row:row + 128,
                                                half * 512:(half + 1) * 512],
                          ot[:, half * 512:(half + 1) * 512],
                          reads=[ot], writes=[t["out"]], pw=True)




class _St:
    pass


def build_program(phases, dev_extra_outputs=()):
    nc = bass.Bass("TRN2", target_bir_lowering=False)
    produced = set()
    for ph in phases:
        produced.update(PHASE_IO[ph][1])
    needed = set()
    for ph in phases:
        needed.update(PHASE_IO[ph][0])
    t = {}
    for name, (shape, dtype, kind) in SCHEMA.items():
        if kind == "in":
            if name in needed:
                t[name] = Buf(nc.dram_tensor(name, shape, dtype, kind="ExternalInput").ap(), name)
        elif kind == "out":
            if name in produced or name in needed:
                t[name] = Buf(nc.dram_tensor(name, shape, dtype, kind="ExternalOutput").ap(), name)
        else:
            if name in produced and (name in dev_extra_outputs):
                t[name] = Buf(nc.dram_tensor(name, shape, dtype, kind="ExternalOutput").ap(), name)
            elif name in produced:
                t[name] = Buf(nc.dram_tensor(name, shape, dtype).ap(), name)
            elif name in needed:
                t[name] = Buf(nc.dram_tensor(name, shape, dtype, kind="ExternalInput").ap(), name)
    if DBG_QT is not None:
        for name, (shape, dtype) in DBG_SCHEMA.items():
            t[name] = Buf(nc.dram_tensor(name, shape, dtype, kind="ExternalOutput").ap(), name)
    if DBG_P6:
        for name in ("dbg_u", "dbg_v", "dbg_w"):
            t[name] = Buf(nc.dram_tensor(name, [8, 128, 32 * 512], BF16, kind="ExternalOutput").ap(), name)
    with ExitStack() as stack:
        p = P(nc, stack)
        st = _St()
        emit_setup(p, st)
        p.phase_barrier()
        st.setup_stack.close()
        merged_pw = ("pw_quant" in phases and "pw_cast" in phases
                     and "p1_l0" in phases)
        for ph in phases:
            if merged_pw and ph in ("pw_quant", "pw_cast"):
                continue
            if ph == "pw_quant":
                emit_pw_quant(p, t)
            elif ph == "pw_cast":
                emit_pw_cast(p, t, part="all")
            elif ph == "p1_l0":
                if merged_pw:
                    pw_stack = ExitStack()
                    pwq_jobs = build_pw_quant_jobs(p, t, pw_stack)
                    pwc_steps = build_pwc_steps(p, t)

                    def fdr(ti):
                        if ti == 0:
                            emit_pw_cast(p, t, part="casts")
                        for _ in range(2):
                            if pwq_jobs:
                                pwq_jobs.pop(0)()
                        if ti >= 4:
                            for _ in range(4):
                                if pwc_steps:
                                    pwc_steps.pop(0)()
                    emit_p1(p, t, st, 0, feeder=fdr)
                    for job in pwq_jobs:
                        job()
                    for s in pwc_steps:
                        s()
                    p.phase_barrier()
                    t.pop("_pwc_stack").close()
                    t.pop("_pwc_stg")
                    pw_stack.close()
                    continue
                emit_p1(p, t, st, 0)
            elif ph == "p1_l1":
                vt_stack = ExitStack()
                t["_p2vt"] = Buf(vt_stack.enter_context(
                    nc.sbuf_tensor("p2vt_pre", [128, 32 * 1024], BF16)), "p2vt_pre")
                t["_p2vt_stack"] = vt_stack
                VQ = ["sync", "gpsimd", "scalar"]

                def fdr1(ti):
                    if ti % 4 == 0 and ti >= 4:
                        for k2 in range(ti - 4, ti):
                            p.dma(VQ[k2 % 3],
                                  t["_p2vt"][:, k2 * 1024:(k2 + 1) * 1024],
                                  t["vt1"].ap()[k2 * 128:(k2 + 1) * 128, :],
                                  reads=[t["vt1"]], writes=[t["_p2vt"]], pw=True)
                emit_p1(p, t, st, 1, feeder=fdr1)
                for k2 in range(28, 32):
                    p.dma(VQ[k2 % 3],
                          t["_p2vt"][:, k2 * 1024:(k2 + 1) * 1024],
                          t["vt1"].ap()[k2 * 128:(k2 + 1) * 128, :],
                          reads=[t["vt1"]], writes=[t["_p2vt"]], pw=True)
            elif ph == "p2_l0":
                emit_p2(p, t, st, 0)
            elif ph == "p2_l1":
                emit_p2(p, t, st, 1)
            elif ph == "p3_l0":
                emit_p3(p, t, st)
            elif ph == "p6_l1":
                emit_p6(p, t, st)
            else:
                raise NotImplementedError(ph)
            if not (ph == "pw_quant" and "pw_cast" in phases):
                p.phase_barrier()
            if ph == "p2_l1" and "_p2vt_stack" in t:
                t.pop("_p2vt_stack").close()
                t.pop("_p2vt")
            if ph == "pw_cast" and "_pwc_stack" in t:
                t.pop("_pwc_stack").close()
                t.pop("_pwc_stg")
        p.flush()
        final = p.barrier_ev()
        def fin(sy):
            for (e, v, _) in final:
                sy.wait_ge(p.sem[e], v)
            return sy.nop()
        p.block.sync(lambda sy: fin(sy))
    return nc, t


_PROG_CACHE = {}


def _full_program():
    if "full" not in _PROG_CACHE:
        _PROG_CACHE["full"] = build_program(PHASE_ORDER)
    return _PROG_CACHE["full"]


_IN_MAP_NAMES = {
    "w_attn": "L0_attn_w_attn", "b_attn": "L0_attn_b_attn",
    "w_proj": "L0_attn_w_proj", "b_proj": "L0_attn_b_proj",
    "w_gate": "L0_ffn_w_gate", "w_up": "L0_ffn_w_up", "w_down": "L0_ffn_w_down",
    "qkv_w": "L1_attn_qkv_w", "qkv_b": "L1_attn_qkv_b",
    "out_w": "L1_attn_out_w", "out_b": "L1_attn_out_b",
    "w1_w": "L1_ffn_w1_w", "w1_s": "L1_ffn_w1_s",
    "w2_w": "L1_ffn_w2_w", "w2_s": "L1_ffn_w2_s",
    "h1_w": "L1_ffn_h1_w", "h1_s": "L1_ffn_h1_s",
    "h2_w": "L1_ffn_h2_w", "h2_s": "L1_ffn_h2_s",
    "w3_w": "L1_ffn_w3_w", "w3_s": "L1_ffn_w3_s",
}


def kernel(**inputs):
    nc, _ = _full_program()
    x = np.ascontiguousarray(inputs["L0_attn_x"])
    shared = {k: np.ascontiguousarray(inputs[v]) for k, v in _IN_MAP_NAMES.items()}
    in_maps = []
    for core in range(NCORES):
        m = dict(shared)
        m["x0"] = np.ascontiguousarray(x[core])
        in_maps.append(m)
    res = run_bass_kernel_spmd(nc, in_maps, core_ids=list(range(NCORES)))
    return np.stack([r["out"] for r in res.results], axis=0)
